# version 27
# baseline (speedup 1.0000x reference)
import sys
sys.path.insert(0, "/opt/trn_rl_repo")
from contextlib import ExitStack

import numpy as np
import ml_dtypes

import concourse.bass as bass
import concourse.mybir as mybir
from concourse.bass_utils import run_bass_kernel_spmd

F32 = mybir.dt.float32
BF16 = mybir.dt.bfloat16
AF = mybir.ActivationFunctionType
ALU = mybir.AluOpType
AX = mybir.AxisListType

B, S, D, I, H = 8, 4096, 1024, 2816, 16
ND = D // 128
NI = I // 128
TW = 512
NT = S // TW
NC4 = TW // 128
RC = 1.5 * 2.0 ** 23
QMAX = 127.0
LN_EPS = 1e-5
ENGINES = ("tensor", "vector", "scalar", "gpsimd", "sync")


class Prog:

    def __init__(self):
        self.ops = {e: [] for e in ENGINES}
        self.cnt = {"t": 0, "v": 0, "s": 0, "g": 0, "dy": 0, "dg": 0, "cc": 0}
        self.eng_sem = {"tensor": "t", "vector": "v", "scalar": "s", "gpsimd": "g"}

    def do(self, eng, fn, waits=None, inc=None, drain=True, force=False):
        w = dict(waits) if waits else {}
        if inc is not None:
            self.cnt[inc[0]] += inc[1]
        self.ops[eng].append([fn, w, inc, drain, force])

    def tick(self, eng, fn, waits=None, drain=True, force=False):
        sem = self.eng_sem[eng]
        if fn is None:
            entry = self.ops[eng][-1]
            assert entry[2] is None and not waits
            self.cnt[sem] += 1
            entry[2] = (sem, 1)
            entry[3] = drain
            return self.cnt[sem]
        self.do(eng, fn, waits, inc=(sem, 1), drain=drain, force=force)
        return self.cnt[sem]

    def dma(self, eng, fn, waits=None):
        sem = "dy" if eng == "sync" else "dg"
        self.do(eng, fn, waits, inc=(sem, 16), drain=False)
        return self.cnt[sem]

    def mark(self, sem):
        return self.cnt[sem]

    def nop(self, eng, waits):
        self.do(eng, lambda e: e.nop(), waits, drain=False)


def build(debug=False):
    nc = bass.Bass("TRN2", target_bir_lowering=False,
                   detect_race_conditions=False)

    def din(name, shape, dt=F32):
        return nc.dram_tensor(name, shape, dt, kind="ExternalInput").ap()

    x_h = din("x", [S, D])
    wT_h = {nm: din(nm + "T", [D, D]) for nm in ("wq", "wv", "wg", "wo")}
    wmod_h = din("wmod", [D, 6 * D])
    cvec_h = din("cvec", [D])
    decay_h = din("decay_rep", [D])
    lng_h = din("lng", [D])
    lnb_h = din("lnb", [D])
    clip_h = din("clips", [2])
    wg_h = din("wg", [I, D])
    wu_h = din("wu", [I, D])
    wd_h = din("wd", [D, I])
    identf_h = din("identf", [128, 128])
    identb_h = din("identb", [128, 128], BF16)
    onesf_h = din("onesf", [1, 128])

    out_h = nc.dram_tensor("out", [S, D], F32, kind="ExternalOutput").ap()

    def dram(name, shape, dt):
        kind = "ExternalOutput" if debug else "Internal"
        return nc.dram_tensor(name, shape, dt, kind=kind).ap()

    x2_sp = dram("x2_sp", [S, D], F32)
    h2_sp = dram("h2_sp", [D, S], F32)
    m_sp = dram("m_sp", [I, S], BF16)
    wgq_sp = dram("wgq_sp", [D, I], BF16)
    wuq_sp = dram("wuq_sp", [D, I], BF16)
    wdq_sp = dram("wdq_sp", [I, D], BF16)
    if debug:
        dbg_modw = dram("dbg_modw", [128, 48], F32)
        dbg_sc = dram("dbg_sc", [128, 64], F32)
    cc1_in = nc.dram_tensor("cc1_in", [1, 1], F32).ap()
    cc1_out = nc.dram_tensor("cc1_out", [1, 1], F32).ap()
    cc2_in = nc.dram_tensor("cc2_in", [1, 1], F32).ap()
    cc2_out = nc.dram_tensor("cc2_out", [1, 1], F32).ap()

    P = Prog()

    with ExitStack() as ctx:
        def sb(name, shape, dt):
            return ctx.enter_context(nc.sbuf_tensor(name, shape, dt))

        axp = [sb("ax_a", [128, 4096], F32), sb("ax_b", [128, 4096], F32)]
        ahp = [sb("ah_a", [128, 4096], BF16), sb("ah_b", [128, 4096], BF16)]
        ar_lny = sb("ar_lny", [128, 4096], BF16)
        ar_y = sb("ar_y", [128, 4096], BF16)
        ar_h1f = sb("ar_h1f", [128, 4096], BF16)
        ar_g = sb("ar_g", [128, 4096], BF16)
        ar_z = sb("ar_z", [128, 4096], BF16)
        ar_ao = sb("ar_ao", [128, 4096], BF16)
        w_attn = {nm: sb(nm + "_sb", [128, ND * D], BF16)
                  for nm in ("wq", "wv", "wg", "wo")}
        scr = sb("scr", [128, 5632], F32)
        wrawb = [scr[:, 0:2816], scr[:, 2816:5632]]
        wmod_bf = scr[:, 0:3072].bitcast(BF16)

        lam8 = sb("lam8", [128, ND], F32)
        csb = sb("csb", [128, ND], F32)
        siluc = sb("siluc", [128, ND], BF16)
        modv = sb("modv", [128, 48], F32)
        modw = sb("modw", [128, 48], F32)
        gam = sb("gam", [128, ND], F32)
        bet = sb("bet", [128, ND], F32)
        stats = sb("stats", [128, 108], F32)
        sgarb = sb("sgarb", [128, 1024], BF16)
        state = sb("state", [128, 512], F32)
        carry = sb("carry", [128, ND], F32)
        t512 = sb("t512", [128, 512], F32)
        t512b = sb("t512b", [128, 512], F32)
        hmaxc = sb("hmaxc", [128, 64], F32)
        mmaxc = sb("mmaxc", [128, 176], F32)
        wq_scale = sb("wq_scale", [128, NI], F32)
        wu_scale = sb("wu_scale", [128, NI], F32)
        wd_scale = sb("wd_scale", [128, ND], F32)
        sc_gate = sb("sc_gate", [128, NI], F32)
        sc_up = sb("sc_up", [128, NI], F32)
        sc_down = sb("sc_down", [128, ND], F32)
        wtmp = sb("wtmp", [128, 3], F32)
        rc_col = sb("rc_col", [128, 1], F32)
        smal = sb("smal", [1, 16], F32)
        bc2 = sb("bc2", [128, 2], F32)
        bc2b = sb("bc2b", [128, 2], F32)
        identf = sb("identf_sb", [128, 128], F32)
        identb = sb("identb_sb", [128, 128], BF16)
        onesf = sb("onesf_sb", [1, 128], F32)
        clips = sb("clips_sb", [1, 2], F32)
        sigc = sb("sigc_sb", [1, 2], F32)
        clips128 = sb("clips128", [128, 2], F32)
        sg128 = sb("sg128", [128, 2], F32)
        sgq = sb("sgq", [128, 2], F32)

        psb = [ctx.enter_context(nc.psum_tensor(f"ps{k}", [128, 512], F32))
               for k in range(8)]

        sems = {nm: ctx.enter_context(nc.semaphore("sem_" + nm))
                for nm in ("t", "v", "s", "g", "dy", "dg", "cc")}
        nc.const_aps.register(F32, RC, rc_col[:, 0:1])

        xq = {}
        for dc in range(ND):
            nm = ("wq", "wv", "wg", "wo")[dc // 2]
            base = (dc % 2) * 4096
            xq[dc] = w_attn[nm][:, base:base + 4096]

        gq_parts = [axp[0].ap().bitcast(BF16), axp[1].ap().bitcast(BF16),
                    ahp[0].ap(), ahp[1].ap()]
        gq_bounds = [0, 8192, 16384, 20480, 24576]

        def gq_flat(idx, ln):
            for pi in range(4):
                if idx < gq_bounds[pi + 1]:
                    off = idx - gq_bounds[pi]
                    assert off + ln <= gq_bounds[pi + 1] - gq_bounds[pi]
                    return gq_parts[pi][:, off:off + ln]
            raise AssertionError

        uq_parts = [ar_lny.ap(), ar_y.ap(), ar_h1f.ap(), ar_g.ap(), ar_z.ap(),
                    ar_ao.ap()]

        def uq_flat(idx, ln):
            part, off = divmod(idx, 4096)
            assert off + ln <= 4096
            return uq_parts[part][:, off:off + ln]

        def flat_segments(lo, ln, flatfn):
            segs = []
            cur = lo
            while cur < lo + ln:
                nxt = min(lo + ln, (cur // 4096 + 1) * 4096)
                segs.append((cur, nxt - cur))
                cur = nxt
            return [(o, l, flatfn(o, l)) for (o, l) in segs]

        def wgqT(hc, lo, ln):
            return gq_flat(hc * I + lo, ln)

        def wuqT(hc, lo, ln):
            return uq_flat(hc * I + lo, ln)

        def wdqT(ic, lo, ln):
            return gq_flat(ic * D + lo, ln)

        def m_in(ic):
            return uq_flat(ic * 512, 512)

        def hq_par(p, ic):
            if p == 0:
                return uq_flat(11264 + ic * 512, 512)
            off = ic * 512
            if off < 8192:
                return w_attn["wv"][:, off:off + 512]
            return w_attn["wo"][:, off - 8192:off - 8192 + 512]

        do_fm = w_attn["wq"].ap()[:, 0:4096]
        x3_tm = w_attn["wg"].ap().bitcast(F32)
        x2t = scr[:, 0:4096]

        def TT(out, a, b, op):
            return lambda e: e.tensor_tensor(out, a, b, op)

        def TS(out, a, s1, s2, op0, op1=None):
            if op1 is None:
                return lambda e: e.tensor_scalar(out, a, s1, None, op0)
            return lambda e: e.tensor_scalar(out, a, s1, s2, op0, op1)

        def TSA(out, a, s1, op0, op1, accum):
            return lambda e: e.tensor_scalar(out, a, s1, None, op0, op1,
                                             accum_out=accum)

        def ACT(out, a, func, scale=1.0, bias=0.0, accum=None):
            return lambda e: e.activation(out, a, func, scale=scale, bias=bias,
                                          accum_out=accum)

        def MM(out, lhsT, rhs, start, stop):
            return lambda e: e.matmul(out, lhsT, rhs, start=start, stop=stop)

        def TRANS(out, in_, ident):
            return lambda e: e.transpose(out, in_, ident)

        def RED(out, in_, op, absval=False):
            return lambda e: e.tensor_reduce(out, in_, AX.X, op,
                                             apply_absolute_value=absval)

        def DMA(out, in_):
            return lambda e: e.dma_start(out=out, in_=in_)

        def DMAS(out, in_):
            return lambda e: e.dma_start(out=out, in_=in_,
                                         allow_slow_non_contiguous=True)

        def col1(ap1d, lo):
            return ap1d[lo:lo + 128].rearrange("(p a) -> p a", a=1)

        P.dma("sync", DMA(identf[:, :], identf_h))
        P.dma("sync", DMA(identb[:, :], identb_h))
        P.dma("sync", DMA(onesf[:, :], onesf_h))
        P.dma("sync", DMA(clips[:, :], clip_h.rearrange("(a b) -> a b", a=1)))
        P.dma("sync", DMAS(clips128[:, :],
                           clip_h.rearrange("(a b) -> a b", a=1).broadcast_to([128, 2])))
        for dc in range(ND):
            P.dma("sync", DMAS(lam8[:, dc:dc + 1], col1(decay_h, dc * 128)))
            P.dma("sync", DMAS(csb[:, dc:dc + 1], col1(cvec_h, dc * 128)))
            P.dma("sync", DMAS(gam[:, dc:dc + 1], col1(lng_h, dc * 128)))
            P.dma("sync", DMAS(bet[:, dc:dc + 1], col1(lnb_h, dc * 128)))
        small_ld = P.mark("dy")

        wm_bufs = [(ar_lny, ar_y, 0), (ar_h1f, ar_y, 2048),
                   (ar_g, ar_ao, 0), (ar_z, ar_ao, 2048)]
        wm_ld = {}

        def wmod_load(dc, w):
            a_t, b_t, boff = wm_bufs[dc % 4]
            P.dma("gpsimd", DMA(a_t[:, 0:4096],
                                wmod_h[dc * 128:(dc + 1) * 128, 0:4096]),
                  waits=w)
            wm_ld[dc] = P.dma("gpsimd", DMA(
                b_t[:, boff:boff + 2048],
                wmod_h[dc * 128:(dc + 1) * 128, 4096:6144]))

        for dc in range(4):
            wmod_load(dc, None)

        for nm in ("wq", "wv", "wg", "wo"):
            for dc in range(ND):
                P.dma("gpsimd", DMA(w_attn[nm][:, dc * D:(dc + 1) * D],
                                    wT_h[nm][dc * 128:(dc + 1) * 128, :]))
        attnw_ld = P.mark("dg")

        P.tick("scalar", ACT(siluc[:, :], csb[:, :], AF.Silu),
               waits={"dy": small_ld})
        P.tick("scalar", ACT(sg128[:, :], clips128[:, :], AF.Sigmoid),
               force=True)
        sil_ready = P.tick("scalar", ACT(sigc[:, :], clips[:, :], AF.Sigmoid))
        sgq_v = P.tick("vector", TS(sgq[:, :], sg128[:, :], 1.0 / QMAX, None,
                                    ALU.mult), waits={"s": sil_ready},
                       force=True)

        P.tick("vector", lambda e: e.memset(rc_col[:, :], RC))
        carry0 = P.tick("vector", lambda e: e.memset(carry[:, :], 0.0),
                        force=True)
        carry_g = {ec: 0 for ec in range(ND)}

        ps_mod = psb[7]
        for dc in range(ND):
            a_t, b_t, boff = wm_bufs[dc % 4]
            w0 = {"dg": wm_ld[dc], "s": sil_ready}
            for jt in range(48):
                if jt < 32:
                    lhs = a_t[:, jt * 128:(jt + 1) * 128]
                else:
                    lhs = b_t[:, boff + (jt - 32) * 128:boff + (jt - 31) * 128]
                P.do("tensor", MM(ps_mod[:, jt:jt + 1], lhs,
                                  siluc[:, dc:dc + 1],
                                  dc == 0 and jt == 0,
                                  dc == ND - 1 and jt == 47), waits=w0)
                w0 = None
            mt = P.tick("tensor", None)
            if dc + 4 < ND:
                wmod_load(dc + 4, {"t": mt})
        mod_t = P.mark("t")
        mod_evac = P.tick("scalar", ACT(modv[:, :], ps_mod[:, 0:48], AF.Identity),
                          waits={"t": mod_t})
        P.tick("vector", TS(modw[:, 0:8], modv[:, 0:8], 0.0, None, ALU.add),
               waits={"s": mod_evac}, force=True)
        P.tick("vector", TS(modw[:, 8:16], modv[:, 8:16], 1.0, None, ALU.add),
               force=True)
        P.tick("vector", TS(modw[:, 24:32], modv[:, 24:32], 0.0, None, ALU.add),
               force=True)
        P.tick("vector", TS(modw[:, 32:40], modv[:, 32:40], 1.0, None, ALU.add),
               force=True)
        for src, dst in ((modv[:, 16:24], modw[:, 16:24]),
                         (modv[:, 40:48], modw[:, 40:48])):
            P.tick("vector", TS(stats[:, 72:80], src, 0.0, None, ALU.is_ge),
                   force=True)
            P.tick("vector", TS(stats[:, 72:80], stats[:, 72:80], 2.0, 1.0,
                                ALU.mult, ALU.subtract), force=True)
            P.tick("vector", TS(stats[:, 80:88], src, 0.25, -0.25,
                                ALU.min, ALU.max), force=True)
            P.tick("vector", TT(dst, src, stats[:, 80:88], ALU.subtract),
                   force=True)
            P.tick("vector", lambda e, dst=dst: e.scalar_tensor_tensor(
                dst, stats[:, 72:80], 0.25, dst, ALU.mult, ALU.add), force=True)
        mod_ready = P.mark("v")
        if debug:
            P.dma("sync", DMA(dbg_modw, modw[:, :]), waits={"v": mod_ready})
        sa_c = modw[:, 0:8]; ca1_c = modw[:, 8:16]; ga_c = modw[:, 16:24]
        sb_c = modw[:, 24:32]; cb1_c = modw[:, 32:40]; gb_c = modw[:, 40:48]

        bank_free = {k: ("v", 0) for k in range(8)}
        bank_free[7] = ("s", mod_evac)

        def bank_wait(bk, w):
            sem, cntv = bank_free[bk]
            w[sem] = max(w.get(sem, 0), cntv)
            return w

        wq_jobs = ([(wg_h, c, D, wq_scale, wgq_sp) for c in range(NI)] +
                   [(wu_h, c, D, wu_scale, wuq_sp) for c in range(NI)] +
                   [(wd_h, c, I, wd_scale, wdq_sp) for c in range(ND)])
        wq_buf_state = [{"t": 0, "dy": P.mark("dy")},
                        {"t": 0, "dy": P.mark("dy")}]
        wq_lw = {}

        def wq_load(j):
            w_h, c_i, row_len, scale_tile, sp_h = wq_jobs[j]
            st_ = wq_buf_state[j % 2]
            wq_lw[j] = P.dma("sync", DMA(wrawb[j % 2][:, 0:row_len],
                                         w_h[c_i * 128:(c_i + 1) * 128, :]),
                             waits={"t": st_["t"], "dy": st_["dy"]})

        def wq_compute(j):
            w_h, c_i, row_len, scale_tile, sp_h = wq_jobs[j]
            buf = wrawb[j % 2][:, 0:row_len]
            n_tp = row_len // 128
            P.tick("vector", RED(wtmp[:, 0:1], buf, ALU.max, absval=True),
                   waits={"dy": wq_lw[j]}, force=True)
            P.tick("vector", TS(scale_tile[:, c_i:c_i + 1], wtmp[:, 0:1],
                                1.0 / QMAX, None, ALU.mult), force=True)
            P.tick("vector", lambda e: e.reciprocal(wtmp[:, 1:2], wtmp[:, 0:1]),
                   force=True)
            P.tick("vector", TS(wtmp[:, 2:3], wtmp[:, 1:2], QMAX, None,
                                ALU.mult), force=True)
            P.tick("vector", lambda e, buf=buf: e.tensor_scalar(
                buf, buf, wtmp[:, 2:3], rc_col[:, 0:1], ALU.mult, ALU.add),
                force=True)
            qv = P.tick("vector", TS(buf, buf, RC, None, ALU.subtract))
            wevac = t512b.ap().bitcast(BF16)
            ev_guard = {0: P.mark("dy"), 1: P.mark("dy")}
            for rnd in range((n_tp + 3) // 4):
                tiles = list(range(rnd * 4, min(n_tp, rnd * 4 + 4)))
                bk = 6 + (rnd % 2)
                w0 = bank_wait(bk, {"v": qv, "dy": small_ld})
                for k, tp in enumerate(tiles):
                    P.do("tensor", TRANS(psb[bk][:, k * 128:(k + 1) * 128],
                                         buf[:, tp * 128:(tp + 1) * 128],
                                         identf[:, :]), waits=w0)
                    w0 = None
                tm = P.tick("tensor", None)
                half = rnd % 2
                ev = P.tick("scalar", ACT(wevac[:, half * 512:half * 512 + len(tiles) * 128],
                                          psb[bk][:, 0:len(tiles) * 128],
                                          AF.Identity),
                            waits={"t": tm, "dy": ev_guard[half]})
                bank_free[bk] = ("s", ev)
                for k, tp in enumerate(tiles):
                    P.dma("sync", DMA(sp_h[tp * 128:(tp + 1) * 128,
                                           c_i * 128:(c_i + 1) * 128],
                                      wevac[:, half * 512 + k * 128:half * 512 + (k + 1) * 128]),
                          waits={"s": ev})
                ev_guard[half] = P.mark("dy")
            wq_buf_state[j % 2] = {"t": P.mark("t"), "dy": P.mark("dy")}

        jobs_per_tile = [7, 7, 7, 7, 7, 7, 7, 3]
        assert sum(jobs_per_tile) == len(wq_jobs)
        wq_load(0)
        wq_load(1)

        def ln_stats(src_tile, col0, rstd_col, negmur_col, pre_wait):
            c_s1, c_s2, c_mu, c_e2, c_mm, c_var, c_rv = (col0 + k for k in range(7))
            first = dict(pre_wait)
            sfirst = {k: v for k, v in pre_wait.items() if k != "v"}
            for tc in range(NC4):
                sl_src = src_tile[:, tc * 1024:(tc + 1) * 1024]
                P.tick("vector", TSA(sl_src, sl_src, 1.0, ALU.mult, ALU.add,
                                     stats[:, 4 * c_s1 + tc:4 * c_s1 + tc + 1]),
                       waits=first, force=True)
                first = None
                P.tick("scalar", ACT(sgarb[:, :], sl_src, AF.Square,
                                     accum=stats[:, 4 * c_s2 + tc:4 * c_s2 + tc + 1]),
                       waits=sfirst)
                sfirst = None
            sm = P.mark("s")
            sl = lambda c: stats[:, 4 * c:4 * c + 4]
            P.tick("vector", TS(sl(c_mu), sl(c_s1), 1.0 / D, None, ALU.mult),
                   waits={"s": sm}, force=True)
            P.tick("vector", TS(sl(c_e2), sl(c_s2), 1.0 / D, None, ALU.mult),
                   force=True)
            P.tick("vector", TT(sl(c_mm), sl(c_mu), sl(c_mu), ALU.mult),
                   force=True)
            P.tick("vector", TT(sl(c_var), sl(c_e2), sl(c_mm), ALU.subtract),
                   force=True)
            P.tick("vector", TS(sl(c_var), sl(c_var), LN_EPS, None, ALU.add),
                   force=True)
            rvm = P.tick("vector", lambda e, c_rv=c_rv, c_var=c_var: e.reciprocal(
                stats[:, 4 * c_rv:4 * c_rv + 4], stats[:, 4 * c_var:4 * c_var + 4]),
                force=True)
            sq = P.tick("scalar", ACT(sl(rstd_col), sl(c_rv), AF.Sqrt),
                        waits={"v": rvm})
            nm_ = P.tick("vector", lambda e: e.scalar_tensor_tensor(
                sl(negmur_col), sl(c_mu), -1.0, sl(rstd_col), ALU.mult, ALU.mult),
                waits={"s": sq}, force=True)
            return nm_

        axp_v = {0: 0, 1: 0}
        ahp_t = {0: 0, 1: 0}
        a4s_prev = sil_ready
        lnys_prev = sil_ready
        h2ps_prev = sil_ready
        x2sp_dy = 0

        for T in range(NT):
            par = T % 2
            ax = axp[par]
            ah = ahp[par]

            xw = {"v": axp_v[par]}
            for tc in range(NC4):
                P.dma("sync", DMA(ax[:, tc * 1024:(tc + 1) * 1024],
                                  x_h[T * TW + tc * 128:T * TW + (tc + 1) * 128, :]),
                      waits=xw)
                xw = None
            x_ld = P.mark("dy")

            j0 = sum(jobs_per_tile[:T])
            jq = list(range(j0, j0 + jobs_per_tile[T]))

            def wq_step(n=1):
                for _ in range(n):
                    if not jq:
                        return
                    j = jq.pop(0)
                    wq_compute(j)
                    if j + 2 < len(wq_jobs):
                        wq_load(j + 2)

            wq_step(2)

            st1 = ln_stats(ax, 0, 7, 8, {"dy": x_ld, "s": a4s_prev})

            def apply_split(dst, src, sccol, bcol, w_extra):
                wv = dict(w_extra)
                ws = dict(w_extra)
                for tc in range(NC4):
                    sc_ap = stats[:, sccol + tc:sccol + tc + 1]
                    b_ap = stats[:, bcol + tc:bcol + tc + 1]
                    sl_d = dst[:, tc * 1024:(tc + 1) * 1024]
                    sl_s = src[:, tc * 1024:(tc + 1) * 1024]
                    if tc % 2 == 0:
                        P.tick("vector", lambda e, sl_d=sl_d, sl_s=sl_s,
                               sc_ap=sc_ap, b_ap=b_ap: e.tensor_scalar(
                            sl_d, sl_s, sc_ap, b_ap, ALU.mult, ALU.add),
                            waits=wv, drain=False)
                        wv = None
                    else:
                        P.tick("scalar", ACT(sl_d, sl_s, AF.Identity,
                                             scale=sc_ap, bias=b_ap),
                               waits=ws, drain=False)
                        ws = None
                return P.mark("v"), P.mark("s")

            h1p_v, h1p_s = apply_split(ah, ax, 28, 32,
                                       {"v": st1, "t": ahp_t[par]})
            a4s_prev = h1p_s

            for dc in range(ND):
                bk = dc % 2
                psv = psb[bk].ap().bitcast(BF16)
                w0 = bank_wait(bk, {"s": h1p_s, "v": h1p_v, "dy": small_ld})
                for tc in range(NC4):
                    P.do("tensor", TRANS(
                        psv[:, tc * 128:(tc + 1) * 128],
                        ah[:, tc * 1024 + dc * 128:tc * 1024 + (dc + 1) * 128],
                        identb[:, :]), waits=w0)
                    w0 = None
                tm = P.tick("tensor", None)
                ev = P.tick("scalar", ACT(
                    ar_h1f[:, dc * 512:(dc + 1) * 512], psv[:, 0:512],
                    AF.Identity, scale=ca1_c[:, dc:dc + 1],
                    bias=sa_c[:, dc:dc + 1]), waits={"t": tm}, drain=False)
                bank_free[bk] = ("s", ev)
            h1f_v = P.mark("s")
            wq_step(1)

            for ec in range(ND):
                w0 = bank_wait(2, {"dg": attnw_ld, "s": h1f_v})
                for dc in range(ND):
                    P.do("tensor", MM(
                        psb[2][:, :],
                        w_attn["wv"][:, dc * D + ec * 128:dc * D + (ec + 1) * 128],
                        ar_h1f[:, dc * 512:(dc + 1) * 512],
                        dc == 0, dc == ND - 1), waits=w0)
                    w0 = None
                tmv = P.tick("tensor", None)
                sc_m = P.tick("vector", lambda e, ec=ec: e.tensor_tensor_scan(
                    state[:, :], lam8[:, ec:ec + 1].broadcast_to([128, 512]),
                    psb[2][:, :], carry[:, ec:ec + 1], ALU.mult, ALU.add),
                    waits={"t": tmv, "v": carry0, "g": carry_g[ec]}, force=True)
                carry_g[ec] = P.tick("gpsimd", lambda e, ec=ec: e.tensor_copy(
                    carry[:, ec:ec + 1], state[:, 511:512]), waits={"v": sc_m})
                bank_free[2] = ("v", sc_m)

                w0 = bank_wait(3, {})
                for dc in range(ND):
                    P.do("tensor", MM(
                        psb[3][:, :],
                        w_attn["wq"][:, dc * D + ec * 128:dc * D + (ec + 1) * 128],
                        ar_h1f[:, dc * 512:(dc + 1) * 512],
                        dc == 0, dc == ND - 1), waits=w0)
                    w0 = None
                tmq = P.tick("tensor", None)
                yv = P.tick("vector", lambda e, ec=ec: e.tensor_tensor(
                    ar_y[:, ec * 512:(ec + 1) * 512], psb[3][:, :], state[:, :],
                    ALU.mult), waits={"t": tmq}, drain=False)
                bank_free[3] = ("v", yv)

                w0 = bank_wait(4, {})
                for dc in range(ND):
                    P.do("tensor", MM(
                        psb[4][:, :],
                        w_attn["wg"][:, dc * D + ec * 128:dc * D + (ec + 1) * 128],
                        ar_h1f[:, dc * 512:(dc + 1) * 512],
                        dc == 0, dc == ND - 1), waits=w0)
                    w0 = None
                tmg = P.tick("tensor", None)
                gs = P.tick("scalar", ACT(ar_g[:, ec * 512:(ec + 1) * 512],
                                          psb[4][:, :], AF.Silu),
                            waits={"t": tmg})
                bank_free[4] = ("s", gs)
            y_v = P.mark("v")
            wq_step(2)

            for tc in range(NC4):
                for half in range(2):
                    bk = half
                    w0 = bank_wait(bk, {"v": y_v})
                    psv = psb[bk].ap().bitcast(BF16)
                    for k in range(4):
                        ec = half * 4 + k
                        P.do("tensor", TRANS(
                            psv[:, k * 128:(k + 1) * 128],
                            ar_y[:, ec * 512 + tc * 128:ec * 512 + (tc + 1) * 128],
                            identb[:, :]), waits=w0)
                        w0 = None
                    tm = P.tick("tensor", None)
                    ev = P.tick("vector", lambda e, tc=tc, half=half, psv=psv,
                                ah=ah: e.tensor_copy(
                        ah[:, tc * 1024 + half * 512:tc * 1024 + (half + 1) * 512],
                        psv[:, 0:512]), waits={"t": tm}, drain=False)
                    bank_free[bk] = ("v", ev)
            ytm_s = P.mark("v")

            st2 = ln_stats(ah, 9, 16, 17, {"v": ytm_s, "s": lnys_prev})
            lny_v, lny_s = apply_split(ar_lny, ah, 64, 68, {"v": st2})
            lnys_prev = lny_s

            psv5 = psb[5].ap().bitcast(BF16)
            for ec in range(ND):
                w0 = bank_wait(5, {"s": lny_s, "v": lny_v, "dy": small_ld})
                for tc in range(NC4):
                    P.do("tensor", TRANS(
                        psv5[:, tc * 128:(tc + 1) * 128],
                        ar_lny[:, tc * 1024 + ec * 128:tc * 1024 + (ec + 1) * 128],
                        identb[:, :]), waits=w0)
                    w0 = None
                tm = P.tick("tensor", None)
                P.tick("vector", lambda e, ec=ec, psv5=psv5: e.tensor_scalar(
                    t512[:, :], psv5[:, 0:512],
                    gam[:, ec:ec + 1], bet[:, ec:ec + 1],
                    ALU.mult, ALU.add), waits={"t": tm}, drain=False)
                zv = P.tick("vector", lambda e, ec=ec: e.tensor_tensor(
                    ar_z[:, ec * 512:(ec + 1) * 512], t512[:, :],
                    ar_g[:, ec * 512:(ec + 1) * 512], ALU.mult), drain=False)
                bank_free[5] = ("v", zv)
            z_v = P.mark("v")
            wq_step(1)

            for ec in range(ND):
                w0 = bank_wait(5, {"v": z_v})
                for dc in range(ND):
                    P.do("tensor", MM(
                        psb[5][:, :],
                        w_attn["wo"][:, dc * D + ec * 128:dc * D + (ec + 1) * 128],
                        ar_z[:, dc * 512:(dc + 1) * 512],
                        dc == 0, dc == ND - 1), waits=w0)
                    w0 = None
                tm = P.tick("tensor", None)
                aov = P.tick("scalar", ACT(
                    ar_ao[:, ec * 512:(ec + 1) * 512], psb[5][:, :],
                    AF.Identity, scale=ga_c[:, ec:ec + 1]), waits={"t": tm},
                    drain=False)
                bank_free[5] = ("s", aov)
            ao_v = P.mark("s")

            for tc in range(NC4):
                for half in range(2):
                    bk = half
                    w0 = bank_wait(bk, {"s": ao_v})
                    psv = psb[bk].ap().bitcast(BF16)
                    for k in range(4):
                        ec = half * 4 + k
                        P.do("tensor", TRANS(
                            psv[:, k * 128:(k + 1) * 128],
                            ar_ao[:, ec * 512 + tc * 128:ec * 512 + (tc + 1) * 128],
                            identb[:, :]), waits=w0)
                        w0 = None
                    tm = P.tick("tensor", None)
                    ev = P.tick("vector",
                                lambda e, tc=tc, half=half, psv=psv, ax=ax: e.tensor_tensor(
                        ax[:, tc * 1024 + half * 512:tc * 1024 + (half + 1) * 512],
                        ax[:, tc * 1024 + half * 512:tc * 1024 + (half + 1) * 512],
                        psv[:, 0:512], ALU.add), waits={"t": tm}, drain=False)
                    bank_free[bk] = ("v", ev)
            x2_v = P.mark("v")
            wq_step(2)

            for tc in range(NC4):
                P.dma("sync", DMA(x2_sp[T * TW + tc * 128:T * TW + (tc + 1) * 128, :],
                                  ax[:, tc * 1024:(tc + 1) * 1024]),
                      waits={"v": x2_v})
            x2sp_dy = P.mark("dy")
            st3 = ln_stats(ax, 18, 25, 26, {"v": x2_v, "s": h2ps_prev})
            h2p_v, h2p_s = apply_split(ah, ax, 100, 104, {"v": st3})
            h2ps_prev = h2p_s

            hv = 0
            for dc in range(ND):
                bk = 2 + (dc % 2)
                psv = psb[bk].ap().bitcast(BF16)
                w0 = bank_wait(bk, {"s": h2p_s, "v": h2p_v})
                for tc in range(NC4):
                    P.do("tensor", TRANS(
                        psv[:, tc * 128:(tc + 1) * 128],
                        ah[:, tc * 1024 + dc * 128:tc * 1024 + (dc + 1) * 128],
                        identb[:, :]), waits=w0)
                    w0 = None
                tm = P.tick("tensor", None)
                hvs = P.tick("scalar", ACT(
                    ax[:, dc * 512:(dc + 1) * 512], psv[:, 0:512],
                    AF.Identity, scale=cb1_c[:, dc:dc + 1],
                    bias=sb_c[:, dc:dc + 1]),
                    waits={"t": tm, "dy": x2sp_dy})
                bank_free[bk] = ("s", hvs)
                hv = P.tick("vector", RED(hmaxc[:, T * 8 + dc:T * 8 + dc + 1],
                                          ax[:, dc * 512:(dc + 1) * 512],
                                          ALU.max, absval=True),
                            waits={"s": hvs})
                P.dma("sync", DMA(h2_sp[dc * 128:(dc + 1) * 128,
                                        T * TW:(T + 1) * TW],
                                  ax[:, dc * 512:(dc + 1) * 512]),
                      waits={"v": hv})
            axp_v[par] = hv
            ahp_t[par] = P.mark("t")

        endA = {"t": P.mark("t"), "v": P.mark("v"), "s": P.mark("s"),
                "dy": P.mark("dy")}

        def flushV():
            return P.tick("vector", TS(state[:, 0:128], state[:, 0:128], 1.0,
                                       None, ALU.mult), force=True)

        P.tick("vector", RED(stats[:, 0:1], hmaxc[:, :], ALU.max), force=True)
        hm = flushV()
        P.do("tensor", TRANS(psb[7][0:1, 0:128], stats[:, 0:1], identf[:, :]),
             waits=bank_wait(7, {"v": hm, "t": endA["t"]}))
        tmx = P.tick("tensor", None)
        P.tick("vector", RED(smal[:, 0:1], psb[7][0:1, 0:128], ALU.max),
               waits={"t": tmx}, force=True)
        gm = flushV()
        c1 = P.dma("gpsimd", DMA(cc1_in, smal[:, 0:1]), waits={"v": gm})
        P.do("gpsimd", lambda e: e.collective_compute(
            "AllReduce", ALU.max, replica_groups=[list(range(8))],
            ins=[cc1_in.opt()], outs=[cc1_out.opt()]),
            waits={"dg": c1}, inc=("cc", 1))
        c1b = P.dma("gpsimd", DMA(smal[:, 1:2], cc1_out),
                    waits={"cc": P.mark("cc")})
        P.do("tensor", MM(psb[7][:, 0:1], onesf[:, :], smal[:, 1:2], True, True),
             waits=bank_wait(7, {"dg": c1b}))
        bm = P.tick("tensor", None)
        P.tick("vector", lambda e: e.tensor_scalar(
            bc2[:, 0:1], psb[7][:, 0:1], sgq[:, 0:1], None, ALU.mult),
            waits={"t": bm, "v": sgq_v}, force=True)
        P.tick("vector", lambda e: e.reciprocal(bc2[:, 1:2], bc2[:, 0:1]),
               force=True)
        bc_s = flushV()
        bank_free[7] = ("v", bc_s)
        P.tick("vector", TS(sc_gate[:, :], wq_scale[:, :], bc2[:, 0:1], None,
                            ALU.mult), force=True)
        P.tick("vector", TS(sc_up[:, :], wu_scale[:, :], bc2[:, 0:1], None,
                            ALU.mult), force=True)
        scv = flushV()

        bp_in = [scr[:, 0:512], scr[:, 512:1024]]
        bp_t = [scr[:, 1024:1536], scr[:, 1536:2048]]
        bp_q1 = {0: 0, 1: 0}
        bp_p3 = {0: bc_s, 1: bc_s}
        xq_tile = {}
        first_dy = {"dy": endA["dy"], "t": endA["t"]}
        for T in range(NT):
            for dc in range(ND):
                k = (T * ND + dc) % 2
                w = dict(first_dy); first_dy = {}
                w["s"] = bp_q1[k]
                lw = P.dma("sync", DMA(
                    bp_in[k], h2_sp[dc * 128:(dc + 1) * 128,
                                    T * TW:(T + 1) * TW]), waits=w)
                bp_q1[k] = P.tick("scalar", ACT(bp_t[k], bp_in[k], AF.Identity,
                                                scale=bc2[:, 1:2], bias=RC),
                                  waits={"dy": lw, "v": bp_p3[k]})
                P.tick("vector", TS(bp_t[k], bp_t[k], RC, QMAX,
                                    ALU.subtract, ALU.min),
                       waits={"s": bp_q1[k]}, drain=False)
                bp_p3[k] = P.tick("vector", TS(
                    xq[dc][:, T * 512:(T + 1) * 512], bp_t[k],
                    -QMAX - 1.0, None, ALU.max), drain=False)
            xq_tile[T] = P.mark("v")
        xq_v = P.mark("v")

        first = {"v": endA["v"], "t": endA["t"], "dy": endA["dy"]}
        for hc in range(ND):
            for (o, l, dst) in flat_segments(hc * I, I, gq_flat):
                P.dma("gpsimd", DMA(dst, wgq_sp[hc * 128:(hc + 1) * 128,
                                                o - hc * I:o - hc * I + l]),
                      waits=first)
                first = None
        gq_ld_g = P.mark("dg")
        for hc in range(ND):
            for (o, l, dst) in flat_segments(hc * I, I, uq_flat):
                P.dma("gpsimd", DMA(dst, wuq_sp[hc * 128:(hc + 1) * 128,
                                                o - hc * I:o - hc * I + l]))
        gq_ld = P.mark("dg")

        if debug:
            P.dma("gpsimd", DMA(dbg_sc[:, 0:22], wq_scale[:, :]),
                  waits={"v": scv})
            P.dma("gpsimd", DMA(dbg_sc[:, 22:44], wu_scale[:, :]))
            P.dma("gpsimd", DMA(dbg_sc[:, 44:46], bc2[:, :]))
            P.dma("gpsimd", DMA(dbg_sc[:, 46:54], wd_scale[:, :]))
            P.dma("gpsimd", DMA(dbg_sc[0:1, 54:56], smal[:, 0:2]))

        ts_g = [scr[:, 0:512], scr[:, 512:1024]]
        ts_m = [scr[:, 1024 + k * 512:1536 + k * 512] for k in range(4)]
        ts_g_rd = {0: xq_v, 1: xq_v}
        ts_m_sp = {k: 0 for k in range(4)}
        first_w = {"dg": gq_ld_g}
        for T in range(NT):
            tile_w = {"v": xq_tile[T]}
            for ic in range(NI):
                n = T * NI + ic
                kg = n % 2
                km = n % 4
                bg, bu = kg * 2, kg * 2 + 1
                w0 = dict(first_w); first_w = {}
                w0.update(tile_w); tile_w = {}
                w0 = bank_wait(bg, w0)
                for hc in range(ND):
                    P.do("tensor", MM(psb[bg][:, :], wgqT(hc, ic * 128, 128),
                                      xq[hc][:, T * 512:(T + 1) * 512],
                                      hc == 0, hc == ND - 1), waits=w0)
                    w0 = None
                tmg = P.tick("tensor", None)
                w0 = bank_wait(bu, {"dg": gq_ld})
                for hc in range(ND):
                    P.do("tensor", MM(psb[bu][:, :], wuqT(hc, ic * 128, 128),
                                      xq[hc][:, T * 512:(T + 1) * 512],
                                      hc == 0, hc == ND - 1), waits=w0)
                    w0 = None
                tmu = P.tick("tensor", None)
                gsil = P.tick("scalar", ACT(ts_g[kg], psb[bg][:, :], AF.Silu,
                                            scale=sc_gate[:, ic:ic + 1]),
                              waits={"t": tmg, "v": ts_g_rd[kg]})
                bank_free[bg] = ("s", gsil)
                mv_ = P.tick("vector", lambda e, ic=ic, kg=kg, km=km, bu=bu:
                             e.scalar_tensor_tensor(
                    ts_m[km], psb[bu][:, :], sc_up[:, ic:ic + 1], ts_g[kg],
                    ALU.mult, ALU.mult),
                    waits={"t": tmu, "s": gsil, "v": scv,
                           "dg": ts_m_sp[km]})
                ts_g_rd[kg] = mv_
                bank_free[bu] = ("v", mv_)
                red = P.tick("vector", RED(
                    mmaxc[:, T * NI + ic:T * NI + ic + 1], ts_m[km],
                    ALU.max, absval=True), drain=False)
                ts_m_sp[km] = P.dma("gpsimd", DMA(
                    m_sp[ic * 128:(ic + 1) * 128, T * TW:(T + 1) * TW],
                    ts_m[km]), waits={"v": red})
        bmid_t = P.mark("t")
        bmid_done = {"t": bmid_t, "v": P.mark("v"), "s": P.mark("s"),
                     "dg": P.mark("dg")}

        P.tick("vector", RED(stats[:, 0:1], mmaxc[:, :], ALU.max), force=True)
        hm2 = flushV()
        P.do("tensor", TRANS(psb[7][0:1, 0:128], stats[:, 0:1], identf[:, :]),
             waits=bank_wait(7, {"v": hm2}))
        tmx2 = P.tick("tensor", None)
        P.tick("vector", RED(smal[:, 5:6], psb[7][0:1, 0:128], ALU.max),
               waits={"t": tmx2}, force=True)
        gm2 = flushV()
        c2 = P.dma("gpsimd", DMA(cc2_in, smal[:, 5:6]), waits={"v": gm2})
        P.do("gpsimd", lambda e: e.collective_compute(
            "AllReduce", ALU.max, replica_groups=[list(range(8))],
            ins=[cc2_in.opt()], outs=[cc2_out.opt()]),
            waits={"dg": c2}, inc=("cc", 1))
        c2b = P.dma("gpsimd", DMA(smal[:, 6:7], cc2_out),
                    waits={"cc": P.mark("cc")})
        P.do("tensor", MM(psb[7][:, 0:1], onesf[:, :], smal[:, 6:7], True, True),
             waits=bank_wait(7, {"dg": c2b}))
        bm2 = P.tick("tensor", None)
        P.tick("vector", lambda e: e.tensor_scalar(
            bc2b[:, 0:1], psb[7][:, 0:1], sgq[:, 1:2], None, ALU.mult),
            waits={"t": bm2}, force=True)
        P.tick("vector", lambda e: e.reciprocal(bc2b[:, 1:2], bc2b[:, 0:1]),
               force=True)
        bc_s2 = flushV()
        bank_free[7] = ("v", bc_s2)
        P.tick("vector", TS(sc_down[:, :], wd_scale[:, :], bc2b[:, 0:1], None,
                            ALU.mult), force=True)
        P.tick("vector", TT(sc_down[:, :], sc_down[:, :], gb_c[:, :],
                            ALU.mult), force=True)
        scd = flushV()

        firsts = {"gpsimd": {"t": bmid_t}, "sync": {"t": bmid_t}}
        for ic in range(NI):
            qu = "gpsimd" if ic % 2 == 0 else "sync"
            for (o, l, dst) in flat_segments(ic * D, D, gq_flat):
                P.dma(qu, DMA(dst, wdq_sp[ic * 128:(ic + 1) * 128,
                                          o - ic * D:o - ic * D + l]),
                      waits=firsts[qu])
                firsts[qu] = None
        wd_ld = P.mark("dg")
        wd_ld_y = P.mark("dy")

        bd_t = [scr[:, 4096 + k * 512:4608 + k * 512] for k in range(2)]
        bd_q1 = {0: 0, 1: 0}
        hq_rd = {0: bmid_t, 1: bmid_t}
        m_in_s = bmid_done["s"]
        x2rd_first = dict(bmid_done)
        x2rd_first["dy"] = x2sp_dy

        def bd_loads(T):
            lms = []
            if T == 0:
                qu, w = "sync", {"s": bd_state["mins"], "t": bmid_t}
            else:
                qu, w = "gpsimd", {"s": bd_state["mins"]}
            for ic in range(NI):
                lms.append(P.dma(qu, DMA(
                    m_in(ic), m_sp[ic * 128:(ic + 1) * 128,
                                   T * TW:(T + 1) * TW]), waits=w))
                w = {}
            bd_state["lms"][T] = lms
            bd_state["lkey"][T] = "dy" if T == 0 else "dg"

        def bd_quant_ic(Tn, ic):
            par = Tn % 2
            lms = bd_state["lms"][Tn]
            key = bd_state["lkey"][Tn]
            k = ic % 2
            q1 = P.tick("scalar", ACT(bd_t[k], m_in(ic), AF.Identity,
                                      scale=bc2b[:, 1:2], bias=RC),
                        waits={key: lms[ic], "v": max(bc_s2, bd_q1[k])})
            P.tick("vector", TS(bd_t[k], bd_t[k], RC, QMAX,
                                ALU.subtract, ALU.min), waits={"s": q1},
                   drain=False)
            bd_q1[k] = P.tick("vector", lambda e, par=par, ic=ic, k=k:
                              e.tensor_scalar(hq_par(par, ic), bd_t[k],
                                              -QMAX - 1.0, None, ALU.max),
                              waits={"t": hq_rd[par]}, drain=False)
            bd_state["qic"][(Tn, ic)] = bd_q1[k]

        def bd_quant_fin(Tn):
            bd_state["mins"] = P.mark("s")
            bd_state["hqv"][Tn] = P.mark("v")

        def bd_mms(T):
            par = T % 2
            hq_v = bd_state["hqv"][T]
            nq = 0
            for hc in range(ND):
                bk = 2 + 3 * (hc % 2)
                if T == 0 and hc == 0:
                    w0 = bank_wait(bk, {"dg": wd_ld, "dy": wd_ld_y})
                    for ic in range(NI):
                        wmm = dict(w0) if w0 else {}
                        w0 = None
                        wmm["v"] = bd_state["qic"][(0, ic)]
                        P.do("tensor", MM(psb[bk][:, :],
                                          wdqT(ic, hc * 128, 128),
                                          hq_par(par, ic),
                                          ic == 0, ic == NI - 1), waits=wmm)
                else:
                    w0 = bank_wait(bk, {"v": hq_v, "dg": wd_ld,
                                        "dy": wd_ld_y})
                    for ic in range(NI):
                        P.do("tensor", MM(psb[bk][:, :],
                                          wdqT(ic, hc * 128, 128),
                                          hq_par(par, ic),
                                          ic == 0, ic == NI - 1), waits=w0)
                        w0 = None
                tm = P.tick("tensor", None)
                dv = P.tick("vector", lambda e, hc=hc, bk=bk: e.tensor_scalar(
                    do_fm[:, hc * 512:(hc + 1) * 512], psb[bk][:, :],
                    sc_down[:, hc:hc + 1], None, ALU.mult),
                    waits={"t": tm, "v": scd}, drain=False)
                bank_free[bk] = ("v", dv)
                if T + 1 < NT:
                    tgt = NI if hc == ND - 1 else min(NI, 3 * (hc + 1))
                    while nq < tgt:
                        bd_quant_ic(T + 1, nq)
                        nq += 1
            if T + 1 < NT:
                bd_quant_fin(T + 1)
            bd_state["dov"][T] = P.mark("v")
            hq_rd[par] = P.mark("t")

        def bd_tail(T):
            do_v = bd_state["dov"][T]
            if T == 0:
                w = dict(bd_state["x2w"])
                for tc in range(NC4):
                    P.dma("sync", DMA(
                        x2t[:, tc * 1024:(tc + 1) * 1024],
                        x2_sp[tc * 128:(tc + 1) * 128, :]), waits=w)
                    bd_state["x2m"][tc] = P.mark("dy")
                    w = {}
            for tc in range(NC4):
                for half in range(2):
                    bk = 3 + half
                    w0 = bank_wait(bk, {"v": do_v,
                                        "dy": bd_state["x2m"][tc]})
                    psv = psb[bk].ap().bitcast(BF16)
                    for k in range(4):
                        hc = half * 4 + k
                        P.do("tensor", TRANS(
                            psv[:, k * 128:(k + 1) * 128],
                            do_fm[:, hc * 512 + tc * 128:hc * 512 + (tc + 1) * 128],
                            identb[:, :]), waits=w0)
                        w0 = None
                    tm = P.tick("tensor", None)
                    ev = P.tick("vector",
                                lambda e, tc=tc, half=half, psv=psv: e.tensor_tensor(
                        x3_tm[:, tc * 1024 + half * 512:tc * 1024 + (half + 1) * 512],
                        x2t[:, tc * 1024 + half * 512:tc * 1024 + (half + 1) * 512],
                        psv[:, 0:512], ALU.add), waits={"t": tm}, drain=False)
                    bank_free[bk] = ("v", ev)
                ev2 = P.mark("v")
                P.dma("sync", DMA(out_h[T * TW + tc * 128:T * TW + (tc + 1) * 128, :],
                                  x3_tm[:, tc * 1024:(tc + 1) * 1024]),
                      waits={"v": ev2})
                if T + 1 < NT:
                    P.dma("sync", DMA(
                        x2t[:, tc * 1024:(tc + 1) * 1024],
                        x2_sp[(T + 1) * TW + tc * 128:(T + 1) * TW + (tc + 1) * 128, :]),
                        waits={"v": ev2})
                    bd_state["x2m"][tc] = P.mark("dy")

        bd_state = {"x2w": x2rd_first, "x2m": {}, "lms": {}, "lkey": {},
                    "hqv": {}, "dov": {}, "mins": m_in_s, "qic": {}}
        bd_loads(0)
        for ic in range(NI):
            bd_quant_ic(0, ic)
        bd_quant_fin(0)
        for T in range(NT):
            if T + 1 < NT:
                bd_loads(T + 1)
            bd_mms(T)
            bd_tail(T)

        P.nop("sync", {"dy": P.mark("dy"), "dg": P.mark("dg")})

        import bisect
        waited = {nm: set() for nm in sems}
        for e in ENGINES:
            for fn, waits, inc, drain, force in P.ops[e]:
                for semname, val in waits.items():
                    waited[semname].add(val)
        waited = {nm: sorted(v) for nm, v in waited.items()}

        def needed(semname, lo, hi):
            i = bisect.bisect_right(waited[semname], lo)
            return i < len(waited[semname]) and waited[semname][i] <= hi

        eng_state = {e: {} for e in ENGINES}

        def run_engine(eng_obj, name):
            stt = eng_state[name]
            own = P.eng_sem.get(name)
            pending = 0
            committed = 0
            for fn, waits, inc, drain, force in P.ops[name]:
                for semname in sorted(waits):
                    val = waits[semname]
                    if val <= stt.get(semname, 0):
                        continue
                    eng_obj.wait_ge(sems[semname], val)
                    stt[semname] = val
                inst = fn(eng_obj)
                if inc is None:
                    continue
                semname, amt = inc
                if semname == own and amt == 1:
                    pending += 1
                    if force or needed(own, committed, committed + pending):
                        if name in ("scalar", "vector"):
                            inst = eng_obj.drain()
                        inst.then_inc(sems[own], pending)
                        committed += pending
                        pending = 0
                else:
                    inst.then_inc(sems[semname], amt)

        block = ctx.enter_context(nc.Block())

        @block.tensor
        def _(t):
            run_engine(t, "tensor")

        @block.vector
        def _(v):
            run_engine(v, "vector")

        @block.scalar
        def _(s):
            run_engine(s, "scalar")

        @block.gpsimd
        def _(g):
            run_engine(g, "gpsimd")

        @block.sync
        def _(sy):
            run_engine(sy, "sync")

    return nc


def kernel(**inputs):
    xs = np.asarray(inputs["L0_attn_x"], dtype=np.float32)
    decay = np.asarray(inputs["L0_attn_decay"], dtype=np.float32)
    wq = np.asarray(inputs["L0_attn_Wq"], dtype=np.float32)
    wv = np.asarray(inputs["L0_attn_Wv"], dtype=np.float32)
    wo = np.asarray(inputs["L0_attn_Wo"], dtype=np.float32)
    wgt = np.asarray(inputs["L0_attn_Wg"], dtype=np.float32)
    lng = np.asarray(inputs["L0_attn_ln_gamma"], dtype=np.float32)
    lnb = np.asarray(inputs["L0_attn_ln_beta"], dtype=np.float32)
    clip_x = np.asarray(inputs["L0_ffn_clip_x"], dtype=np.float32)
    clip_down = np.asarray(inputs["L0_ffn_clip_down"], dtype=np.float32)
    w_gate = np.asarray(inputs["L0_ffn_w_gate"], dtype=np.float32)
    w_up = np.asarray(inputs["L0_ffn_w_up"], dtype=np.float32)
    w_down = np.asarray(inputs["L0_ffn_w_down"], dtype=np.float32)
    cmat = np.asarray(inputs["L0_c"], dtype=np.float32)
    wmod = np.asarray(inputs["L0_wmod"], dtype=np.float32)

    common = {
        "wqT": np.ascontiguousarray(wq.T), "wvT": np.ascontiguousarray(wv.T),
        "wgT": np.ascontiguousarray(wgt.T), "woT": np.ascontiguousarray(wo.T),
        "wmod": np.ascontiguousarray(wmod),
        "decay_rep": np.ascontiguousarray(np.repeat(decay, D // H)),
        "lng": np.ascontiguousarray(lng), "lnb": np.ascontiguousarray(lnb),
        "clips": np.ascontiguousarray(np.concatenate([clip_x, clip_down])),
        "wg": np.ascontiguousarray(w_gate), "wu": np.ascontiguousarray(w_up),
        "wd": np.ascontiguousarray(w_down),
        "identf": np.eye(128, dtype=np.float32),
        "identb": np.eye(128, dtype=ml_dtypes.bfloat16),
        "onesf": np.ones((1, 128), dtype=np.float32),
    }
    in_maps = []
    for b in range(B):
        m = dict(common)
        m["x"] = np.ascontiguousarray(xs[b])
        m["cvec"] = np.ascontiguousarray(cmat[b])
        in_maps.append(m)

    nc = build()
    res = run_bass_kernel_spmd(nc, in_maps, core_ids=list(range(B)))
    return np.stack([np.asarray(r["out"]) for r in res.results], axis=0)


if __name__ == "__main__":
    print("building program...")
    nc = build()
    print("instructions:", nc.n_instructions())


# version 28 (final)
# speedup vs baseline: 1.0983x; 1.0983x over previous
import sys
sys.path.insert(0, "/opt/trn_rl_repo")
from contextlib import ExitStack

import numpy as np
import ml_dtypes

import concourse.bass as bass
import concourse.mybir as mybir
from concourse.bass_utils import run_bass_kernel_spmd

F32 = mybir.dt.float32
BF16 = mybir.dt.bfloat16
AF = mybir.ActivationFunctionType
ALU = mybir.AluOpType
AX = mybir.AxisListType

B, S, D, I, H = 8, 4096, 1024, 2816, 16
ND = D // 128
NI = I // 128
TW = 512
NT = S // TW
NC4 = TW // 128
RC = 1.5 * 2.0 ** 23
QMAX = 127.0
LN_EPS = 1e-5
ENGINES = ("tensor", "vector", "scalar", "gpsimd", "sync")


class Prog:

    def __init__(self):
        self.ops = {e: [] for e in ENGINES}
        self.cnt = {"t": 0, "v": 0, "s": 0, "g": 0, "dy": 0, "dg": 0, "cc": 0}
        self.eng_sem = {"tensor": "t", "vector": "v", "scalar": "s", "gpsimd": "g"}

    def do(self, eng, fn, waits=None, inc=None, drain=True, force=False):
        w = dict(waits) if waits else {}
        if inc is not None:
            self.cnt[inc[0]] += inc[1]
        self.ops[eng].append([fn, w, inc, drain, force])

    def tick(self, eng, fn, waits=None, drain=True, force=False):
        sem = self.eng_sem[eng]
        if fn is None:
            entry = self.ops[eng][-1]
            assert entry[2] is None and not waits
            self.cnt[sem] += 1
            entry[2] = (sem, 1)
            entry[3] = drain
            return self.cnt[sem]
        self.do(eng, fn, waits, inc=(sem, 1), drain=drain, force=force)
        return self.cnt[sem]

    def dma(self, eng, fn, waits=None):
        sem = "dy" if eng == "sync" else "dg"
        self.do(eng, fn, waits, inc=(sem, 16), drain=False)
        return self.cnt[sem]

    def mark(self, sem):
        return self.cnt[sem]

    def nop(self, eng, waits):
        self.do(eng, lambda e: e.nop(), waits, drain=False)


def build(debug=False):
    nc = bass.Bass("TRN2", target_bir_lowering=False,
                   detect_race_conditions=False)

    def din(name, shape, dt=F32):
        return nc.dram_tensor(name, shape, dt, kind="ExternalInput").ap()

    x_h = din("x", [S, D])
    wT_h = {nm: din(nm + "T", [D, D]) for nm in ("wq", "wv", "wg", "wo")}
    wmod_h = din("wmod", [D, 6 * D])
    cvec_h = din("cvec", [D])
    decay_h = din("decay_rep", [D])
    lng_h = din("lng", [D])
    lnb_h = din("lnb", [D])
    clip_h = din("clips", [2])
    wg_h = din("wg", [I, D])
    wu_h = din("wu", [I, D])
    wd_h = din("wd", [D, I])
    identf_h = din("identf", [128, 128])
    identb_h = din("identb", [128, 128], BF16)
    onesf_h = din("onesf", [1, 128])

    out_h = nc.dram_tensor("out", [S, D], F32, kind="ExternalOutput").ap()

    def dram(name, shape, dt):
        kind = "ExternalOutput" if debug else "Internal"
        return nc.dram_tensor(name, shape, dt, kind=kind).ap()

    x2_sp = dram("x2_sp", [S, D], F32)
    h2_sp = dram("h2_sp", [D, S], F32)
    m_sp = dram("m_sp", [I, S], BF16)
    wgq_sp = dram("wgq_sp", [D, I], BF16)
    wuq_sp = dram("wuq_sp", [D, I], BF16)
    wdq_sp = dram("wdq_sp", [I, D], BF16)
    if debug:
        dbg_modw = dram("dbg_modw", [128, 48], F32)
        dbg_sc = dram("dbg_sc", [128, 64], F32)
    cc1_in = nc.dram_tensor("cc1_in", [1, 1], F32).ap()
    cc1_out = nc.dram_tensor("cc1_out", [1, 1], F32).ap()
    cc2_in = nc.dram_tensor("cc2_in", [1, 1], F32).ap()
    cc2_out = nc.dram_tensor("cc2_out", [1, 1], F32).ap()

    P = Prog()

    with ExitStack() as ctx:
        def sb(name, shape, dt):
            return ctx.enter_context(nc.sbuf_tensor(name, shape, dt))

        axp = [sb("ax_a", [128, 4096], F32), sb("ax_b", [128, 4096], F32)]
        ahp = [sb("ah_a", [128, 4096], BF16), sb("ah_b", [128, 4096], BF16)]
        ar_lny = sb("ar_lny", [128, 4096], BF16)
        ar_y = sb("ar_y", [128, 4096], BF16)
        ar_h1f = sb("ar_h1f", [128, 4096], BF16)
        ar_g = sb("ar_g", [128, 4096], BF16)
        ar_z = sb("ar_z", [128, 4096], BF16)
        ar_ao = sb("ar_ao", [128, 4096], BF16)
        w_attn = {nm: sb(nm + "_sb", [128, ND * D], BF16)
                  for nm in ("wq", "wv", "wg", "wo")}
        scr = sb("scr", [128, 5632], F32)
        wrawb = [scr[:, 0:2816], scr[:, 2816:5632]]
        wmod_bf = scr[:, 0:3072].bitcast(BF16)

        lam8 = sb("lam8", [128, ND], F32)
        csb = sb("csb", [128, ND], F32)
        siluc = sb("siluc", [128, ND], BF16)
        modv = sb("modv", [128, 48], F32)
        modw = sb("modw", [128, 48], F32)
        gam = sb("gam", [128, ND], F32)
        bet = sb("bet", [128, ND], F32)
        stats = sb("stats", [128, 108], F32)
        sgarb = sb("sgarb", [128, 1024], BF16)
        state = sb("state", [128, 512], F32)
        carry = sb("carry", [128, ND], F32)
        t512 = sb("t512", [128, 512], F32)
        t512b = sb("t512b", [128, 512], F32)
        hmaxc = sb("hmaxc", [128, 64], F32)
        mmaxc = sb("mmaxc", [128, 176], F32)
        wq_scale = sb("wq_scale", [128, NI], F32)
        wu_scale = sb("wu_scale", [128, NI], F32)
        wd_scale = sb("wd_scale", [128, ND], F32)
        sc_gate = sb("sc_gate", [128, NI], F32)
        sc_up = sb("sc_up", [128, NI], F32)
        sc_down = sb("sc_down", [128, ND], F32)
        wtmp = sb("wtmp", [128, 3], F32)
        rc_col = sb("rc_col", [128, 1], F32)
        smal = sb("smal", [1, 16], F32)
        bc2 = sb("bc2", [128, 2], F32)
        bc2b = sb("bc2b", [128, 2], F32)
        identf = sb("identf_sb", [128, 128], F32)
        identb = sb("identb_sb", [128, 128], BF16)
        onesf = sb("onesf_sb", [1, 128], F32)
        clips = sb("clips_sb", [1, 2], F32)
        sigc = sb("sigc_sb", [1, 2], F32)
        clips128 = sb("clips128", [128, 2], F32)
        sg128 = sb("sg128", [128, 2], F32)
        sgq = sb("sgq", [128, 2], F32)

        psb = [ctx.enter_context(nc.psum_tensor(f"ps{k}", [128, 512], F32))
               for k in range(8)]

        sems = {nm: ctx.enter_context(nc.semaphore("sem_" + nm))
                for nm in ("t", "v", "s", "g", "dy", "dg", "cc")}
        nc.const_aps.register(F32, RC, rc_col[:, 0:1])

        xq = {}
        for dc in range(ND):
            nm = ("wq", "wv", "wg", "wo")[dc // 2]
            base = (dc % 2) * 4096
            xq[dc] = w_attn[nm][:, base:base + 4096]

        gq_parts = [axp[0].ap().bitcast(BF16), axp[1].ap().bitcast(BF16),
                    ahp[0].ap(), ahp[1].ap()]
        gq_bounds = [0, 8192, 16384, 20480, 24576]

        def gq_flat(idx, ln):
            for pi in range(4):
                if idx < gq_bounds[pi + 1]:
                    off = idx - gq_bounds[pi]
                    assert off + ln <= gq_bounds[pi + 1] - gq_bounds[pi]
                    return gq_parts[pi][:, off:off + ln]
            raise AssertionError

        uq_parts = [ar_lny.ap(), ar_y.ap(), ar_h1f.ap(), ar_g.ap(), ar_z.ap(),
                    ar_ao.ap()]

        def uq_flat(idx, ln):
            part, off = divmod(idx, 4096)
            assert off + ln <= 4096
            return uq_parts[part][:, off:off + ln]

        def flat_segments(lo, ln, flatfn):
            segs = []
            cur = lo
            while cur < lo + ln:
                nxt = min(lo + ln, (cur // 4096 + 1) * 4096)
                segs.append((cur, nxt - cur))
                cur = nxt
            return [(o, l, flatfn(o, l)) for (o, l) in segs]

        def wgqT(hc, lo, ln):
            return gq_flat(hc * I + lo, ln)

        def wuqT(hc, lo, ln):
            return uq_flat(hc * I + lo, ln)

        def wdqT(ic, lo, ln):
            return gq_flat(ic * D + lo, ln)

        def m_in(ic):
            return uq_flat(ic * 512, 512)

        def hq_par(p, ic):
            if p == 0:
                return uq_flat(11264 + ic * 512, 512)
            off = ic * 512
            if off < 8192:
                return w_attn["wv"][:, off:off + 512]
            return w_attn["wo"][:, off - 8192:off - 8192 + 512]

        do_fm = w_attn["wq"].ap()[:, 0:4096]
        x3_tm = w_attn["wg"].ap().bitcast(F32)
        x2t = scr[:, 0:4096]

        def TT(out, a, b, op):
            return lambda e: e.tensor_tensor(out, a, b, op)

        def TS(out, a, s1, s2, op0, op1=None):
            if op1 is None:
                return lambda e: e.tensor_scalar(out, a, s1, None, op0)
            return lambda e: e.tensor_scalar(out, a, s1, s2, op0, op1)

        def TSA(out, a, s1, op0, op1, accum):
            return lambda e: e.tensor_scalar(out, a, s1, None, op0, op1,
                                             accum_out=accum)

        def ACT(out, a, func, scale=1.0, bias=0.0, accum=None):
            return lambda e: e.activation(out, a, func, scale=scale, bias=bias,
                                          accum_out=accum)

        def MM(out, lhsT, rhs, start, stop):
            return lambda e: e.matmul(out, lhsT, rhs, start=start, stop=stop)

        def TRANS(out, in_, ident):
            return lambda e: e.transpose(out, in_, ident)

        def RED(out, in_, op, absval=False):
            return lambda e: e.tensor_reduce(out, in_, AX.X, op,
                                             apply_absolute_value=absval)

        def DMA(out, in_):
            return lambda e: e.dma_start(out=out, in_=in_)

        def DMAS(out, in_):
            return lambda e: e.dma_start(out=out, in_=in_,
                                         allow_slow_non_contiguous=True)

        def col1(ap1d, lo):
            return ap1d[lo:lo + 128].rearrange("(p a) -> p a", a=1)

        P.dma("sync", DMA(identf[:, :], identf_h))
        P.dma("sync", DMA(identb[:, :], identb_h))
        P.dma("sync", DMA(onesf[:, :], onesf_h))
        P.dma("sync", DMA(clips[:, :], clip_h.rearrange("(a b) -> a b", a=1)))
        P.dma("sync", DMAS(clips128[:, :],
                           clip_h.rearrange("(a b) -> a b", a=1).broadcast_to([128, 2])))
        for dc in range(ND):
            P.dma("sync", DMAS(lam8[:, dc:dc + 1], col1(decay_h, dc * 128)))
            P.dma("sync", DMAS(csb[:, dc:dc + 1], col1(cvec_h, dc * 128)))
            P.dma("sync", DMAS(gam[:, dc:dc + 1], col1(lng_h, dc * 128)))
            P.dma("sync", DMAS(bet[:, dc:dc + 1], col1(lnb_h, dc * 128)))
        small_ld = P.mark("dy")

        wm_bufs = [(ar_lny, ar_y, 0), (ar_h1f, ar_y, 2048),
                   (ar_g, ar_ao, 0), (ar_z, ar_ao, 2048)]
        wm_ld = {}

        def wmod_load(dc, w):
            a_t, b_t, boff = wm_bufs[dc % 4]
            P.dma("gpsimd", DMA(a_t[:, 0:4096],
                                wmod_h[dc * 128:(dc + 1) * 128, 0:4096]),
                  waits=w)
            wm_ld[dc] = P.dma("gpsimd", DMA(
                b_t[:, boff:boff + 2048],
                wmod_h[dc * 128:(dc + 1) * 128, 4096:6144]))

        for dc in range(4):
            wmod_load(dc, None)

        for nm in ("wq", "wv", "wg", "wo"):
            for dc in range(ND):
                P.dma("gpsimd", DMA(w_attn[nm][:, dc * D:(dc + 1) * D],
                                    wT_h[nm][dc * 128:(dc + 1) * 128, :]))
        attnw_ld = P.mark("dg")

        P.tick("scalar", ACT(siluc[:, :], csb[:, :], AF.Silu),
               waits={"dy": small_ld})
        P.tick("scalar", ACT(sg128[:, :], clips128[:, :], AF.Sigmoid),
               force=True)
        sil_ready = P.tick("scalar", ACT(sigc[:, :], clips[:, :], AF.Sigmoid))
        sgq_v = P.tick("vector", TS(sgq[:, :], sg128[:, :], 1.0 / QMAX, None,
                                    ALU.mult), waits={"s": sil_ready},
                       force=True)

        P.tick("vector", lambda e: e.memset(rc_col[:, :], RC))
        carry0 = P.tick("vector", lambda e: e.memset(carry[:, :], 0.0),
                        force=True)
        carry_g = {ec: 0 for ec in range(ND)}

        ps_mod = psb[7]
        for dc in range(ND):
            a_t, b_t, boff = wm_bufs[dc % 4]
            w0 = {"dg": wm_ld[dc], "s": sil_ready}
            for jt in range(48):
                if jt < 32:
                    lhs = a_t[:, jt * 128:(jt + 1) * 128]
                else:
                    lhs = b_t[:, boff + (jt - 32) * 128:boff + (jt - 31) * 128]
                P.do("tensor", MM(ps_mod[:, jt:jt + 1], lhs,
                                  siluc[:, dc:dc + 1],
                                  dc == 0 and jt == 0,
                                  dc == ND - 1 and jt == 47), waits=w0)
                w0 = None
            mt = P.tick("tensor", None)
            if dc + 4 < ND:
                wmod_load(dc + 4, {"t": mt})
        mod_t = P.mark("t")
        mod_evac = P.tick("scalar", ACT(modv[:, :], ps_mod[:, 0:48], AF.Identity),
                          waits={"t": mod_t})
        P.tick("vector", TS(modw[:, 0:8], modv[:, 0:8], 0.0, None, ALU.add),
               waits={"s": mod_evac}, force=True)
        P.tick("vector", TS(modw[:, 8:16], modv[:, 8:16], 1.0, None, ALU.add),
               force=True)
        P.tick("vector", TS(modw[:, 24:32], modv[:, 24:32], 0.0, None, ALU.add),
               force=True)
        P.tick("vector", TS(modw[:, 32:40], modv[:, 32:40], 1.0, None, ALU.add),
               force=True)
        for src, dst in ((modv[:, 16:24], modw[:, 16:24]),
                         (modv[:, 40:48], modw[:, 40:48])):
            P.tick("vector", TS(stats[:, 72:80], src, 0.0, None, ALU.is_ge),
                   force=True)
            P.tick("vector", TS(stats[:, 72:80], stats[:, 72:80], 2.0, 1.0,
                                ALU.mult, ALU.subtract), force=True)
            P.tick("vector", TS(stats[:, 80:88], src, 0.25, -0.25,
                                ALU.min, ALU.max), force=True)
            P.tick("vector", TT(dst, src, stats[:, 80:88], ALU.subtract),
                   force=True)
            P.tick("vector", lambda e, dst=dst: e.scalar_tensor_tensor(
                dst, stats[:, 72:80], 0.25, dst, ALU.mult, ALU.add), force=True)
        mod_ready = P.mark("v")
        if debug:
            P.dma("sync", DMA(dbg_modw, modw[:, :]), waits={"v": mod_ready})
        sa_c = modw[:, 0:8]; ca1_c = modw[:, 8:16]; ga_c = modw[:, 16:24]
        sb_c = modw[:, 24:32]; cb1_c = modw[:, 32:40]; gb_c = modw[:, 40:48]

        bank_free = {k: ("v", 0) for k in range(8)}
        bank_free[7] = ("s", mod_evac)

        def bank_wait(bk, w):
            sem, cntv = bank_free[bk]
            w[sem] = max(w.get(sem, 0), cntv)
            return w

        wq_jobs = ([(wg_h, c, D, wq_scale, wgq_sp) for c in range(NI)] +
                   [(wu_h, c, D, wu_scale, wuq_sp) for c in range(NI)] +
                   [(wd_h, c, I, wd_scale, wdq_sp) for c in range(ND)])
        wq_buf_state = [{"t": 0, "dy": P.mark("dy")},
                        {"t": 0, "dy": P.mark("dy")}]
        wq_lw = {}

        def wq_load(j):
            w_h, c_i, row_len, scale_tile, sp_h = wq_jobs[j]
            st_ = wq_buf_state[j % 2]
            wq_lw[j] = P.dma("sync", DMA(wrawb[j % 2][:, 0:row_len],
                                         w_h[c_i * 128:(c_i + 1) * 128, :]),
                             waits={"t": st_["t"], "dy": st_["dy"]})

        def wq_compute(j):
            w_h, c_i, row_len, scale_tile, sp_h = wq_jobs[j]
            buf = wrawb[j % 2][:, 0:row_len]
            n_tp = row_len // 128
            P.tick("vector", RED(wtmp[:, 0:1], buf, ALU.max, absval=True),
                   waits={"dy": wq_lw[j]}, force=True)
            P.tick("vector", TS(scale_tile[:, c_i:c_i + 1], wtmp[:, 0:1],
                                1.0 / QMAX, None, ALU.mult), force=True)
            P.tick("vector", lambda e: e.reciprocal(wtmp[:, 1:2], wtmp[:, 0:1]),
                   force=True)
            P.tick("vector", TS(wtmp[:, 2:3], wtmp[:, 1:2], QMAX, None,
                                ALU.mult), force=True)
            P.tick("vector", lambda e, buf=buf: e.tensor_scalar(
                buf, buf, wtmp[:, 2:3], rc_col[:, 0:1], ALU.mult, ALU.add),
                force=True)
            qv = P.tick("vector", TS(buf, buf, RC, None, ALU.subtract))
            wevac = t512b.ap().bitcast(BF16)
            ev_guard = {0: P.mark("dy"), 1: P.mark("dy")}
            for rnd in range((n_tp + 3) // 4):
                tiles = list(range(rnd * 4, min(n_tp, rnd * 4 + 4)))
                bk = 6 + (rnd % 2)
                w0 = bank_wait(bk, {"v": qv, "dy": small_ld})
                for k, tp in enumerate(tiles):
                    P.do("tensor", TRANS(psb[bk][:, k * 128:(k + 1) * 128],
                                         buf[:, tp * 128:(tp + 1) * 128],
                                         identf[:, :]), waits=w0)
                    w0 = None
                tm = P.tick("tensor", None)
                half = rnd % 2
                ev = P.tick("scalar", ACT(wevac[:, half * 512:half * 512 + len(tiles) * 128],
                                          psb[bk][:, 0:len(tiles) * 128],
                                          AF.Identity),
                            waits={"t": tm, "dy": ev_guard[half]})
                bank_free[bk] = ("s", ev)
                for k, tp in enumerate(tiles):
                    P.dma("sync", DMA(sp_h[tp * 128:(tp + 1) * 128,
                                           c_i * 128:(c_i + 1) * 128],
                                      wevac[:, half * 512 + k * 128:half * 512 + (k + 1) * 128]),
                          waits={"s": ev})
                ev_guard[half] = P.mark("dy")
            wq_buf_state[j % 2] = {"t": P.mark("t"), "dy": P.mark("dy")}

        jobs_per_tile = [7, 7, 7, 7, 7, 7, 7, 3]
        assert sum(jobs_per_tile) == len(wq_jobs)
        wq_load(0)
        wq_load(1)

        def ln_stats(src_tile, col0, rstd_col, negmur_col, pre_wait):
            c_s1, c_s2, c_mu, c_e2, c_mm, c_var, c_rv = (col0 + k for k in range(7))
            first = dict(pre_wait)
            sfirst = {k: v for k, v in pre_wait.items() if k != "v"}
            for tc in range(NC4):
                sl_src = src_tile[:, tc * 1024:(tc + 1) * 1024]
                P.tick("vector", TSA(sl_src, sl_src, 1.0, ALU.mult, ALU.add,
                                     stats[:, 4 * c_s1 + tc:4 * c_s1 + tc + 1]),
                       waits=first, force=True)
                first = None
                P.tick("scalar", ACT(sgarb[:, :], sl_src, AF.Square,
                                     accum=stats[:, 4 * c_s2 + tc:4 * c_s2 + tc + 1]),
                       waits=sfirst)
                sfirst = None
            sm = P.mark("s")
            sl = lambda c: stats[:, 4 * c:4 * c + 4]
            P.tick("vector", TS(sl(c_mu), sl(c_s1), 1.0 / D, None, ALU.mult),
                   waits={"s": sm}, force=True)
            P.tick("vector", TS(sl(c_e2), sl(c_s2), 1.0 / D, None, ALU.mult),
                   force=True)
            P.tick("vector", TT(sl(c_mm), sl(c_mu), sl(c_mu), ALU.mult),
                   force=True)
            P.tick("vector", TT(sl(c_var), sl(c_e2), sl(c_mm), ALU.subtract),
                   force=True)
            P.tick("vector", TS(sl(c_var), sl(c_var), LN_EPS, None, ALU.add),
                   force=True)
            rvm = P.tick("vector", lambda e, c_rv=c_rv, c_var=c_var: e.reciprocal(
                stats[:, 4 * c_rv:4 * c_rv + 4], stats[:, 4 * c_var:4 * c_var + 4]),
                force=True)
            sq = P.tick("scalar", ACT(sl(rstd_col), sl(c_rv), AF.Sqrt),
                        waits={"v": rvm})
            nm_ = P.tick("vector", lambda e: e.scalar_tensor_tensor(
                sl(negmur_col), sl(c_mu), -1.0, sl(rstd_col), ALU.mult, ALU.mult),
                waits={"s": sq}, force=True)
            return nm_

        axp_v = {0: 0, 1: 0}
        ahp_t = {0: 0, 1: 0}
        a4s_prev = sil_ready
        lnys_prev = sil_ready
        h2ps_prev = sil_ready
        x2sp_dy = 0

        for T in range(NT):
            par = T % 2
            ax = axp[par]
            ah = ahp[par]

            xw = {"v": axp_v[par]}
            for tc in range(NC4):
                P.dma("sync", DMA(ax[:, tc * 1024:(tc + 1) * 1024],
                                  x_h[T * TW + tc * 128:T * TW + (tc + 1) * 128, :]),
                      waits=xw)
                xw = None
            x_ld = P.mark("dy")

            j0 = sum(jobs_per_tile[:T])
            jq = list(range(j0, j0 + jobs_per_tile[T]))

            def wq_step(n=1):
                for _ in range(n):
                    if not jq:
                        return
                    j = jq.pop(0)
                    wq_compute(j)
                    if j + 2 < len(wq_jobs):
                        wq_load(j + 2)

            wq_step(2)

            st1 = ln_stats(ax, 0, 7, 8, {"dy": x_ld, "s": a4s_prev})

            def apply_split(dst, src, sccol, bcol, w_extra):
                wv = dict(w_extra)
                ws = dict(w_extra)
                for tc in range(NC4):
                    sc_ap = stats[:, sccol + tc:sccol + tc + 1]
                    b_ap = stats[:, bcol + tc:bcol + tc + 1]
                    sl_d = dst[:, tc * 1024:(tc + 1) * 1024]
                    sl_s = src[:, tc * 1024:(tc + 1) * 1024]
                    if tc % 2 == 0:
                        P.tick("vector", lambda e, sl_d=sl_d, sl_s=sl_s,
                               sc_ap=sc_ap, b_ap=b_ap: e.tensor_scalar(
                            sl_d, sl_s, sc_ap, b_ap, ALU.mult, ALU.add),
                            waits=wv, drain=False)
                        wv = None
                    else:
                        P.tick("scalar", ACT(sl_d, sl_s, AF.Identity,
                                             scale=sc_ap, bias=b_ap),
                               waits=ws, drain=False)
                        ws = None
                return P.mark("v"), P.mark("s")

            h1p_v, h1p_s = apply_split(ah, ax, 28, 32,
                                       {"v": st1, "t": ahp_t[par]})
            a4s_prev = h1p_s

            for dc in range(ND):
                bk = dc % 2
                psv = psb[bk].ap().bitcast(BF16)
                w0 = bank_wait(bk, {"s": h1p_s, "v": h1p_v, "dy": small_ld})
                for tc in range(NC4):
                    P.do("tensor", TRANS(
                        psv[:, tc * 128:(tc + 1) * 128],
                        ah[:, tc * 1024 + dc * 128:tc * 1024 + (dc + 1) * 128],
                        identb[:, :]), waits=w0)
                    w0 = None
                tm = P.tick("tensor", None)
                ev = P.tick("scalar", ACT(
                    ar_h1f[:, dc * 512:(dc + 1) * 512], psv[:, 0:512],
                    AF.Identity, scale=ca1_c[:, dc:dc + 1],
                    bias=sa_c[:, dc:dc + 1]), waits={"t": tm}, drain=False)
                bank_free[bk] = ("s", ev)
            h1f_v = P.mark("s")
            wq_step(1)

            for ec in range(ND):
                w0 = bank_wait(2, {"dg": attnw_ld, "s": h1f_v})
                for dc in range(ND):
                    P.do("tensor", MM(
                        psb[2][:, :],
                        w_attn["wv"][:, dc * D + ec * 128:dc * D + (ec + 1) * 128],
                        ar_h1f[:, dc * 512:(dc + 1) * 512],
                        dc == 0, dc == ND - 1), waits=w0)
                    w0 = None
                tmv = P.tick("tensor", None)
                sc_m = P.tick("vector", lambda e, ec=ec: e.tensor_tensor_scan(
                    state[:, :], lam8[:, ec:ec + 1].broadcast_to([128, 512]),
                    psb[2][:, :], carry[:, ec:ec + 1], ALU.mult, ALU.add),
                    waits={"t": tmv, "v": carry0, "g": carry_g[ec]}, force=True)
                carry_g[ec] = P.tick("gpsimd", lambda e, ec=ec: e.tensor_copy(
                    carry[:, ec:ec + 1], state[:, 511:512]), waits={"v": sc_m})
                bank_free[2] = ("v", sc_m)

                w0 = bank_wait(3, {})
                for dc in range(ND):
                    P.do("tensor", MM(
                        psb[3][:, :],
                        w_attn["wq"][:, dc * D + ec * 128:dc * D + (ec + 1) * 128],
                        ar_h1f[:, dc * 512:(dc + 1) * 512],
                        dc == 0, dc == ND - 1), waits=w0)
                    w0 = None
                tmq = P.tick("tensor", None)
                yv = P.tick("vector", lambda e, ec=ec: e.tensor_tensor(
                    ar_y[:, ec * 512:(ec + 1) * 512], psb[3][:, :], state[:, :],
                    ALU.mult), waits={"t": tmq}, drain=False)
                bank_free[3] = ("v", yv)

                w0 = bank_wait(4, {})
                for dc in range(ND):
                    P.do("tensor", MM(
                        psb[4][:, :],
                        w_attn["wg"][:, dc * D + ec * 128:dc * D + (ec + 1) * 128],
                        ar_h1f[:, dc * 512:(dc + 1) * 512],
                        dc == 0, dc == ND - 1), waits=w0)
                    w0 = None
                tmg = P.tick("tensor", None)
                gs = P.tick("scalar", ACT(ar_g[:, ec * 512:(ec + 1) * 512],
                                          psb[4][:, :], AF.Silu),
                            waits={"t": tmg})
                bank_free[4] = ("s", gs)
            y_v = P.mark("v")
            wq_step(2)

            for tc in range(NC4):
                for half in range(2):
                    bk = half
                    w0 = bank_wait(bk, {"v": y_v})
                    psv = psb[bk].ap().bitcast(BF16)
                    for k in range(4):
                        ec = half * 4 + k
                        P.do("tensor", TRANS(
                            psv[:, k * 128:(k + 1) * 128],
                            ar_y[:, ec * 512 + tc * 128:ec * 512 + (tc + 1) * 128],
                            identb[:, :]), waits=w0)
                        w0 = None
                    tm = P.tick("tensor", None)
                    ev = P.tick("vector", lambda e, tc=tc, half=half, psv=psv,
                                ah=ah: e.tensor_copy(
                        ah[:, tc * 1024 + half * 512:tc * 1024 + (half + 1) * 512],
                        psv[:, 0:512]), waits={"t": tm}, drain=False)
                    bank_free[bk] = ("v", ev)
            ytm_s = P.mark("v")

            st2 = ln_stats(ah, 9, 16, 17, {"v": ytm_s, "s": lnys_prev})
            lny_v, lny_s = apply_split(ar_lny, ah, 64, 68, {"v": st2})
            lnys_prev = lny_s

            psv5 = psb[5].ap().bitcast(BF16)
            for ec in range(ND):
                w0 = bank_wait(5, {"s": lny_s, "v": lny_v, "dy": small_ld})
                for tc in range(NC4):
                    P.do("tensor", TRANS(
                        psv5[:, tc * 128:(tc + 1) * 128],
                        ar_lny[:, tc * 1024 + ec * 128:tc * 1024 + (ec + 1) * 128],
                        identb[:, :]), waits=w0)
                    w0 = None
                tm = P.tick("tensor", None)
                P.tick("vector", lambda e, ec=ec, psv5=psv5: e.tensor_scalar(
                    t512[:, :], psv5[:, 0:512],
                    gam[:, ec:ec + 1], bet[:, ec:ec + 1],
                    ALU.mult, ALU.add), waits={"t": tm}, drain=False)
                zv = P.tick("vector", lambda e, ec=ec: e.tensor_tensor(
                    ar_z[:, ec * 512:(ec + 1) * 512], t512[:, :],
                    ar_g[:, ec * 512:(ec + 1) * 512], ALU.mult), drain=False)
                bank_free[5] = ("v", zv)
            z_v = P.mark("v")
            wq_step(1)

            for ec in range(ND):
                w0 = bank_wait(5, {"v": z_v})
                for dc in range(ND):
                    P.do("tensor", MM(
                        psb[5][:, :],
                        w_attn["wo"][:, dc * D + ec * 128:dc * D + (ec + 1) * 128],
                        ar_z[:, dc * 512:(dc + 1) * 512],
                        dc == 0, dc == ND - 1), waits=w0)
                    w0 = None
                tm = P.tick("tensor", None)
                aov = P.tick("scalar", ACT(
                    ar_ao[:, ec * 512:(ec + 1) * 512], psb[5][:, :],
                    AF.Identity, scale=ga_c[:, ec:ec + 1]), waits={"t": tm},
                    drain=False)
                bank_free[5] = ("s", aov)
            ao_v = P.mark("s")

            for tc in range(NC4):
                for half in range(2):
                    bk = half
                    w0 = bank_wait(bk, {"s": ao_v})
                    psv = psb[bk].ap().bitcast(BF16)
                    for k in range(4):
                        ec = half * 4 + k
                        P.do("tensor", TRANS(
                            psv[:, k * 128:(k + 1) * 128],
                            ar_ao[:, ec * 512 + tc * 128:ec * 512 + (tc + 1) * 128],
                            identb[:, :]), waits=w0)
                        w0 = None
                    tm = P.tick("tensor", None)
                    ev = P.tick("vector",
                                lambda e, tc=tc, half=half, psv=psv, ax=ax: e.tensor_tensor(
                        ax[:, tc * 1024 + half * 512:tc * 1024 + (half + 1) * 512],
                        ax[:, tc * 1024 + half * 512:tc * 1024 + (half + 1) * 512],
                        psv[:, 0:512], ALU.add), waits={"t": tm}, drain=False)
                    bank_free[bk] = ("v", ev)
            x2_v = P.mark("v")
            wq_step(2)

            x2sp_marks = []
            for tc in range(NC4):
                P.dma("sync", DMA(x2_sp[T * TW + tc * 128:T * TW + (tc + 1) * 128, :],
                                  ax[:, tc * 1024:(tc + 1) * 1024]),
                      waits={"v": x2_v} if tc == 0 else None)
                x2sp_marks.append(P.mark("dy"))
            x2sp_dy = P.mark("dy")
            st3 = ln_stats(ax, 18, 25, 26, {"v": x2_v, "s": h2ps_prev})
            h2p_v, h2p_s = apply_split(ah, ax, 100, 104, {"v": st3})
            h2ps_prev = h2p_s

            hv = 0
            for dc in range(ND):
                bk = 2 + (dc % 2)
                psv = psb[bk].ap().bitcast(BF16)
                w0 = bank_wait(bk, {"s": h2p_s, "v": h2p_v})
                for tc in range(NC4):
                    P.do("tensor", TRANS(
                        psv[:, tc * 128:(tc + 1) * 128],
                        ah[:, tc * 1024 + dc * 128:tc * 1024 + (dc + 1) * 128],
                        identb[:, :]), waits=w0)
                    w0 = None
                tm = P.tick("tensor", None)
                hvs = P.tick("scalar", ACT(
                    ax[:, dc * 512:(dc + 1) * 512], psv[:, 0:512],
                    AF.Identity, scale=cb1_c[:, dc:dc + 1],
                    bias=sb_c[:, dc:dc + 1]),
                    waits={"t": tm, "dy": x2sp_marks[dc // 2]})
                bank_free[bk] = ("s", hvs)
                hv = P.tick("vector", RED(hmaxc[:, T * 8 + dc:T * 8 + dc + 1],
                                          ax[:, dc * 512:(dc + 1) * 512],
                                          ALU.max, absval=True),
                            waits={"s": hvs})
                P.dma("sync", DMA(h2_sp[dc * 128:(dc + 1) * 128,
                                        T * TW:(T + 1) * TW],
                                  ax[:, dc * 512:(dc + 1) * 512]),
                      waits={"v": hv})
            axp_v[par] = hv
            ahp_t[par] = P.mark("t")

        endA = {"t": P.mark("t"), "v": P.mark("v"), "s": P.mark("s"),
                "dy": P.mark("dy")}

        def flushV():
            return P.tick("vector", TS(state[:, 0:128], state[:, 0:128], 1.0,
                                       None, ALU.mult), force=True)

        P.tick("vector", RED(stats[:, 0:1], hmaxc[:, :], ALU.max), force=True)
        hm = flushV()
        P.do("tensor", TRANS(psb[7][0:1, 0:128], stats[:, 0:1], identf[:, :]),
             waits=bank_wait(7, {"v": hm, "t": endA["t"]}))
        tmx = P.tick("tensor", None)
        P.tick("vector", RED(smal[:, 0:1], psb[7][0:1, 0:128], ALU.max),
               waits={"t": tmx}, force=True)
        gm = flushV()
        c1 = P.dma("gpsimd", DMA(cc1_in, smal[:, 0:1]), waits={"v": gm})
        P.do("gpsimd", lambda e: e.collective_compute(
            "AllReduce", ALU.max, replica_groups=[list(range(8))],
            ins=[cc1_in.opt()], outs=[cc1_out.opt()]),
            waits={"dg": c1}, inc=("cc", 1))
        c1b = P.dma("gpsimd", DMA(smal[:, 1:2], cc1_out),
                    waits={"cc": P.mark("cc")})
        P.do("tensor", MM(psb[7][:, 0:1], onesf[:, :], smal[:, 1:2], True, True),
             waits=bank_wait(7, {"dg": c1b}))
        bm = P.tick("tensor", None)
        P.tick("vector", lambda e: e.tensor_scalar(
            bc2[:, 0:1], psb[7][:, 0:1], sgq[:, 0:1], None, ALU.mult),
            waits={"t": bm, "v": sgq_v}, force=True)
        P.tick("vector", lambda e: e.reciprocal(bc2[:, 1:2], bc2[:, 0:1]),
               force=True)
        bc_s = flushV()
        bank_free[7] = ("v", bc_s)
        P.tick("vector", TS(sc_gate[:, :], wq_scale[:, :], bc2[:, 0:1], None,
                            ALU.mult), force=True)
        P.tick("vector", TS(sc_up[:, :], wu_scale[:, :], bc2[:, 0:1], None,
                            ALU.mult), force=True)
        scv = flushV()

        bp_in = [scr[:, 0:512], scr[:, 512:1024]]
        bp_t = [scr[:, 1024:1536], scr[:, 1536:2048]]
        bp_q1 = {0: 0, 1: 0}
        bp_p3 = {0: bc_s, 1: bc_s}
        xq_tile = {}
        first_dy = {"dy": endA["dy"], "t": endA["t"]}
        for T in range(NT):
            for dc in range(ND):
                k = (T * ND + dc) % 2
                w = dict(first_dy); first_dy = {}
                w["s"] = bp_q1[k]
                lw = P.dma("sync", DMA(
                    bp_in[k], h2_sp[dc * 128:(dc + 1) * 128,
                                    T * TW:(T + 1) * TW]), waits=w)
                bp_q1[k] = P.tick("scalar", ACT(bp_t[k], bp_in[k], AF.Identity,
                                                scale=bc2[:, 1:2], bias=RC),
                                  waits={"dy": lw, "v": bp_p3[k]})
                P.tick("vector", TS(bp_t[k], bp_t[k], RC, QMAX,
                                    ALU.subtract, ALU.min),
                       waits={"s": bp_q1[k]}, drain=False)
                bp_p3[k] = P.tick("vector", TS(
                    xq[dc][:, T * 512:(T + 1) * 512], bp_t[k],
                    -QMAX - 1.0, None, ALU.max), drain=False)
            xq_tile[T] = P.mark("v")
        xq_v = P.mark("v")

        first = {"v": endA["v"], "t": endA["t"], "dy": endA["dy"]}
        for hc in range(ND):
            for (o, l, dst) in flat_segments(hc * I, I, gq_flat):
                P.dma("gpsimd", DMA(dst, wgq_sp[hc * 128:(hc + 1) * 128,
                                                o - hc * I:o - hc * I + l]),
                      waits=first)
                first = None
        gq_ld_g = P.mark("dg")
        for hc in range(ND):
            for (o, l, dst) in flat_segments(hc * I, I, uq_flat):
                P.dma("gpsimd", DMA(dst, wuq_sp[hc * 128:(hc + 1) * 128,
                                                o - hc * I:o - hc * I + l]))
        gq_ld = P.mark("dg")

        if debug:
            P.dma("gpsimd", DMA(dbg_sc[:, 0:22], wq_scale[:, :]),
                  waits={"v": scv})
            P.dma("gpsimd", DMA(dbg_sc[:, 22:44], wu_scale[:, :]))
            P.dma("gpsimd", DMA(dbg_sc[:, 44:46], bc2[:, :]))
            P.dma("gpsimd", DMA(dbg_sc[:, 46:54], wd_scale[:, :]))
            P.dma("gpsimd", DMA(dbg_sc[0:1, 54:56], smal[:, 0:2]))

        ts_g = [scr[:, 0:512], scr[:, 512:1024]]
        ts_m = [scr[:, 1024 + k * 512:1536 + k * 512] for k in range(4)]
        ts_g_rd = {0: xq_v, 1: xq_v}
        ts_m_sp = {k: 0 for k in range(4)}
        first_w = {"dg": gq_ld_g}
        for T in range(NT):
            tile_w = {"v": xq_tile[T]}
            for ic in range(NI):
                n = T * NI + ic
                kg = n % 2
                km = n % 4
                bg, bu = kg * 2, kg * 2 + 1
                w0 = dict(first_w); first_w = {}
                w0.update(tile_w); tile_w = {}
                w0 = bank_wait(bg, w0)
                for hc in range(ND):
                    P.do("tensor", MM(psb[bg][:, :], wgqT(hc, ic * 128, 128),
                                      xq[hc][:, T * 512:(T + 1) * 512],
                                      hc == 0, hc == ND - 1), waits=w0)
                    w0 = None
                tmg = P.tick("tensor", None)
                w0 = bank_wait(bu, {"dg": gq_ld})
                for hc in range(ND):
                    P.do("tensor", MM(psb[bu][:, :], wuqT(hc, ic * 128, 128),
                                      xq[hc][:, T * 512:(T + 1) * 512],
                                      hc == 0, hc == ND - 1), waits=w0)
                    w0 = None
                tmu = P.tick("tensor", None)
                gsil = P.tick("scalar", ACT(ts_g[kg], psb[bg][:, :], AF.Silu,
                                            scale=sc_gate[:, ic:ic + 1]),
                              waits={"t": tmg, "v": ts_g_rd[kg]})
                bank_free[bg] = ("s", gsil)
                mv_ = P.tick("vector", lambda e, ic=ic, kg=kg, km=km, bu=bu:
                             e.scalar_tensor_tensor(
                    ts_m[km], psb[bu][:, :], sc_up[:, ic:ic + 1], ts_g[kg],
                    ALU.mult, ALU.mult),
                    waits={"t": tmu, "s": gsil, "v": scv,
                           "dg": ts_m_sp[km]})
                ts_g_rd[kg] = mv_
                bank_free[bu] = ("v", mv_)
                red = P.tick("vector", RED(
                    mmaxc[:, T * NI + ic:T * NI + ic + 1], ts_m[km],
                    ALU.max, absval=True), drain=False)
                ts_m_sp[km] = P.dma("gpsimd", DMA(
                    m_sp[ic * 128:(ic + 1) * 128, T * TW:(T + 1) * TW],
                    ts_m[km]), waits={"v": red})
        bmid_t = P.mark("t")
        bmid_done = {"t": bmid_t, "v": P.mark("v"), "s": P.mark("s"),
                     "dg": P.mark("dg")}

        P.tick("vector", RED(stats[:, 0:1], mmaxc[:, :], ALU.max), force=True)
        hm2 = flushV()
        P.do("tensor", TRANS(psb[7][0:1, 0:128], stats[:, 0:1], identf[:, :]),
             waits=bank_wait(7, {"v": hm2}))
        tmx2 = P.tick("tensor", None)
        P.tick("vector", RED(smal[:, 5:6], psb[7][0:1, 0:128], ALU.max),
               waits={"t": tmx2}, force=True)
        gm2 = flushV()
        c2 = P.dma("gpsimd", DMA(cc2_in, smal[:, 5:6]), waits={"v": gm2})
        P.do("gpsimd", lambda e: e.collective_compute(
            "AllReduce", ALU.max, replica_groups=[list(range(8))],
            ins=[cc2_in.opt()], outs=[cc2_out.opt()]),
            waits={"dg": c2}, inc=("cc", 1))
        c2b = P.dma("gpsimd", DMA(smal[:, 6:7], cc2_out),
                    waits={"cc": P.mark("cc")})
        P.do("tensor", MM(psb[7][:, 0:1], onesf[:, :], smal[:, 6:7], True, True),
             waits=bank_wait(7, {"dg": c2b}))
        bm2 = P.tick("tensor", None)
        P.tick("vector", lambda e: e.tensor_scalar(
            bc2b[:, 0:1], psb[7][:, 0:1], sgq[:, 1:2], None, ALU.mult),
            waits={"t": bm2}, force=True)
        P.tick("vector", lambda e: e.reciprocal(bc2b[:, 1:2], bc2b[:, 0:1]),
               force=True)
        bc_s2 = flushV()
        bank_free[7] = ("v", bc_s2)
        P.tick("vector", TS(sc_down[:, :], wd_scale[:, :], bc2b[:, 0:1], None,
                            ALU.mult), force=True)
        P.tick("vector", TT(sc_down[:, :], sc_down[:, :], gb_c[:, :],
                            ALU.mult), force=True)
        scd = flushV()

        firsts = {"gpsimd": {"t": bmid_t}, "sync": {"t": bmid_t}}
        for ic in range(NI):
            qu = "gpsimd" if ic % 2 == 0 else "sync"
            for (o, l, dst) in flat_segments(ic * D, D, gq_flat):
                P.dma(qu, DMA(dst, wdq_sp[ic * 128:(ic + 1) * 128,
                                          o - ic * D:o - ic * D + l]),
                      waits=firsts[qu])
                firsts[qu] = None
        wd_ld = P.mark("dg")
        wd_ld_y = P.mark("dy")

        bd_t = [scr[:, 4096 + k * 512:4608 + k * 512] for k in range(2)]
        bd_q1 = {0: 0, 1: 0}
        hq_rd = {0: bmid_t, 1: bmid_t}
        m_in_s = bmid_done["s"]
        x2rd_first = dict(bmid_done)
        x2rd_first["dy"] = x2sp_dy

        def bd_loads(T):
            lms = []
            if T == 0:
                qu, w = "sync", {"s": bd_state["mins"], "t": bmid_t}
            else:
                qu, w = "gpsimd", {"s": bd_state["mins"]}
            for ic in range(NI):
                lms.append(P.dma(qu, DMA(
                    m_in(ic), m_sp[ic * 128:(ic + 1) * 128,
                                   T * TW:(T + 1) * TW]), waits=w))
                w = {}
            bd_state["lms"][T] = lms
            bd_state["lkey"][T] = "dy" if T == 0 else "dg"

        def bd_quant_ic(Tn, ic):
            par = Tn % 2
            lms = bd_state["lms"][Tn]
            key = bd_state["lkey"][Tn]
            k = ic % 2
            q1 = P.tick("scalar", ACT(bd_t[k], m_in(ic), AF.Identity,
                                      scale=bc2b[:, 1:2], bias=RC),
                        waits={key: lms[ic], "v": max(bc_s2, bd_q1[k])})
            P.tick("vector", TS(bd_t[k], bd_t[k], RC, QMAX,
                                ALU.subtract, ALU.min), waits={"s": q1},
                   drain=False)
            bd_q1[k] = P.tick("vector", lambda e, par=par, ic=ic, k=k:
                              e.tensor_scalar(hq_par(par, ic), bd_t[k],
                                              -QMAX - 1.0, None, ALU.max),
                              waits={"t": hq_rd[par]}, drain=False)
            bd_state["qic"][(Tn, ic)] = bd_q1[k]

        def bd_quant_fin(Tn):
            bd_state["mins"] = P.mark("s")
            bd_state["hqv"][Tn] = P.mark("v")

        def bd_mms(T):
            par = T % 2
            hq_v = bd_state["hqv"][T]
            nq = 0
            for hc in range(ND):
                bk = 2 + 3 * (hc % 2)
                if T == 0 and hc == 0:
                    w0 = bank_wait(bk, {"dg": wd_ld, "dy": wd_ld_y})
                    for ic in range(NI):
                        wmm = dict(w0) if w0 else {}
                        w0 = None
                        wmm["v"] = bd_state["qic"][(0, ic)]
                        P.do("tensor", MM(psb[bk][:, :],
                                          wdqT(ic, hc * 128, 128),
                                          hq_par(par, ic),
                                          ic == 0, ic == NI - 1), waits=wmm)
                else:
                    w0 = bank_wait(bk, {"v": hq_v, "dg": wd_ld,
                                        "dy": wd_ld_y})
                    for ic in range(NI):
                        P.do("tensor", MM(psb[bk][:, :],
                                          wdqT(ic, hc * 128, 128),
                                          hq_par(par, ic),
                                          ic == 0, ic == NI - 1), waits=w0)
                        w0 = None
                tm = P.tick("tensor", None)
                dv = P.tick("vector", lambda e, hc=hc, bk=bk: e.tensor_scalar(
                    do_fm[:, hc * 512:(hc + 1) * 512], psb[bk][:, :],
                    sc_down[:, hc:hc + 1], None, ALU.mult),
                    waits={"t": tm, "v": scd}, drain=False)
                bank_free[bk] = ("v", dv)
                if T + 1 < NT:
                    tgt = NI if hc == ND - 1 else min(NI, 3 * (hc + 1))
                    while nq < tgt:
                        bd_quant_ic(T + 1, nq)
                        nq += 1
            if T + 1 < NT:
                bd_quant_fin(T + 1)
            bd_state["dov"][T] = P.mark("v")
            hq_rd[par] = P.mark("t")

        def bd_tail(T):
            do_v = bd_state["dov"][T]
            if T == 0:
                w = dict(bd_state["x2w"])
                for tc in range(NC4):
                    P.dma("sync", DMA(
                        x2t[:, tc * 1024:(tc + 1) * 1024],
                        x2_sp[tc * 128:(tc + 1) * 128, :]), waits=w)
                    bd_state["x2m"][tc] = P.mark("dy")
                    w = {}
            for tc in range(NC4):
                for half in range(2):
                    bk = 3 + half
                    w0 = bank_wait(bk, {"v": do_v,
                                        "dy": bd_state["x2m"][tc]})
                    psv = psb[bk].ap().bitcast(BF16)
                    for k in range(4):
                        hc = half * 4 + k
                        P.do("tensor", TRANS(
                            psv[:, k * 128:(k + 1) * 128],
                            do_fm[:, hc * 512 + tc * 128:hc * 512 + (tc + 1) * 128],
                            identb[:, :]), waits=w0)
                        w0 = None
                    tm = P.tick("tensor", None)
                    ev = P.tick("vector",
                                lambda e, tc=tc, half=half, psv=psv: e.tensor_tensor(
                        x3_tm[:, tc * 1024 + half * 512:tc * 1024 + (half + 1) * 512],
                        x2t[:, tc * 1024 + half * 512:tc * 1024 + (half + 1) * 512],
                        psv[:, 0:512], ALU.add), waits={"t": tm}, drain=False)
                    bank_free[bk] = ("v", ev)
                ev2 = P.mark("v")
                P.dma("sync", DMA(out_h[T * TW + tc * 128:T * TW + (tc + 1) * 128, :],
                                  x3_tm[:, tc * 1024:(tc + 1) * 1024]),
                      waits={"v": ev2})
                if T + 1 < NT:
                    P.dma("sync", DMA(
                        x2t[:, tc * 1024:(tc + 1) * 1024],
                        x2_sp[(T + 1) * TW + tc * 128:(T + 1) * TW + (tc + 1) * 128, :]),
                        waits={"v": ev2})
                    bd_state["x2m"][tc] = P.mark("dy")

        bd_state = {"x2w": x2rd_first, "x2m": {}, "lms": {}, "lkey": {},
                    "hqv": {}, "dov": {}, "mins": m_in_s, "qic": {}}
        bd_loads(0)
        for ic in range(NI):
            bd_quant_ic(0, ic)
        bd_quant_fin(0)
        for T in range(NT):
            if T + 1 < NT:
                bd_loads(T + 1)
            bd_mms(T)
            bd_tail(T)

        P.nop("sync", {"dy": P.mark("dy"), "dg": P.mark("dg")})

        import bisect
        waited = {nm: set() for nm in sems}
        for e in ENGINES:
            for fn, waits, inc, drain, force in P.ops[e]:
                for semname, val in waits.items():
                    waited[semname].add(val)
        waited = {nm: sorted(v) for nm, v in waited.items()}

        def needed(semname, lo, hi):
            i = bisect.bisect_right(waited[semname], lo)
            return i < len(waited[semname]) and waited[semname][i] <= hi

        eng_state = {e: {} for e in ENGINES}

        def run_engine(eng_obj, name):
            stt = eng_state[name]
            own = P.eng_sem.get(name)
            pending = 0
            committed = 0
            for fn, waits, inc, drain, force in P.ops[name]:
                for semname in sorted(waits):
                    val = waits[semname]
                    if val <= stt.get(semname, 0):
                        continue
                    eng_obj.wait_ge(sems[semname], val)
                    stt[semname] = val
                inst = fn(eng_obj)
                if inc is None:
                    continue
                semname, amt = inc
                if semname == own and amt == 1:
                    pending += 1
                    if force or needed(own, committed, committed + pending):
                        if name in ("scalar", "vector"):
                            inst = eng_obj.drain()
                        inst.then_inc(sems[own], pending)
                        committed += pending
                        pending = 0
                else:
                    inst.then_inc(sems[semname], amt)

        block = ctx.enter_context(nc.Block())

        @block.tensor
        def _(t):
            run_engine(t, "tensor")

        @block.vector
        def _(v):
            run_engine(v, "vector")

        @block.scalar
        def _(s):
            run_engine(s, "scalar")

        @block.gpsimd
        def _(g):
            run_engine(g, "gpsimd")

        @block.sync
        def _(sy):
            run_engine(sy, "sync")

    return nc


def kernel(**inputs):
    xs = np.asarray(inputs["L0_attn_x"], dtype=np.float32)
    decay = np.asarray(inputs["L0_attn_decay"], dtype=np.float32)
    wq = np.asarray(inputs["L0_attn_Wq"], dtype=np.float32)
    wv = np.asarray(inputs["L0_attn_Wv"], dtype=np.float32)
    wo = np.asarray(inputs["L0_attn_Wo"], dtype=np.float32)
    wgt = np.asarray(inputs["L0_attn_Wg"], dtype=np.float32)
    lng = np.asarray(inputs["L0_attn_ln_gamma"], dtype=np.float32)
    lnb = np.asarray(inputs["L0_attn_ln_beta"], dtype=np.float32)
    clip_x = np.asarray(inputs["L0_ffn_clip_x"], dtype=np.float32)
    clip_down = np.asarray(inputs["L0_ffn_clip_down"], dtype=np.float32)
    w_gate = np.asarray(inputs["L0_ffn_w_gate"], dtype=np.float32)
    w_up = np.asarray(inputs["L0_ffn_w_up"], dtype=np.float32)
    w_down = np.asarray(inputs["L0_ffn_w_down"], dtype=np.float32)
    cmat = np.asarray(inputs["L0_c"], dtype=np.float32)
    wmod = np.asarray(inputs["L0_wmod"], dtype=np.float32)

    common = {
        "wqT": np.ascontiguousarray(wq.T), "wvT": np.ascontiguousarray(wv.T),
        "wgT": np.ascontiguousarray(wgt.T), "woT": np.ascontiguousarray(wo.T),
        "wmod": np.ascontiguousarray(wmod),
        "decay_rep": np.ascontiguousarray(np.repeat(decay, D // H)),
        "lng": np.ascontiguousarray(lng), "lnb": np.ascontiguousarray(lnb),
        "clips": np.ascontiguousarray(np.concatenate([clip_x, clip_down])),
        "wg": np.ascontiguousarray(w_gate), "wu": np.ascontiguousarray(w_up),
        "wd": np.ascontiguousarray(w_down),
        "identf": np.eye(128, dtype=np.float32),
        "identb": np.eye(128, dtype=ml_dtypes.bfloat16),
        "onesf": np.ones((1, 128), dtype=np.float32),
    }
    in_maps = []
    for b in range(B):
        m = dict(common)
        m["x"] = np.ascontiguousarray(xs[b])
        m["cvec"] = np.ascontiguousarray(cmat[b])
        in_maps.append(m)

    nc = build()
    res = run_bass_kernel_spmd(nc, in_maps, core_ids=list(range(B)))
    return np.stack([np.asarray(r["out"]) for r in res.results], axis=0)


if __name__ == "__main__":
    print("building program...")
    nc = build()
    print("instructions:", nc.n_instructions())
